# version 12 (final)
# speedup vs baseline: 1.1897x; 1.0225x over previous
import numpy as np
import ml_dtypes
import concourse.bass as bass
import concourse.mybir as mybir
from concourse.bass_utils import run_bass_kernel_spmd
from contextlib import ExitStack

F32 = mybir.dt.float32
BF16 = mybir.dt.bfloat16
AF = mybir.ActivationFunctionType
ALU = mybir.AluOpType
AXX = mybir.AxisListType.X
BF = ml_dtypes.bfloat16

D = 2048
T = 4096
M = 4096
NH = 16
NC = D // 128
NT128 = T // 128
NT512 = T // 512
NM = M // 128

RS128 = float(1.0 / np.sqrt(128.0))
RSD = float(1.0 / np.sqrt(2048.0))
ESHIFT = -30.0
EPS_FFN = 1e-7
EPS_LN = 1e-5
MAXNR = 0.99

PHASES = ["P0", "P1", "L2", "Aq", "Ak", "Av", "Az", "B", "C1", "C2", "D1", "D2", "E"]
TAPS = []
SIMPLE_FFN = False
LN_BYPASS = False
SIM_FENCE_ALL = False

ENGS = ("sync", "gpsimd", "tensor", "vector", "scalar")


DMA_OWNER = {"dS": "sync", "dG": "gpsimd"}


class Prog:
    def __init__(self, nc, sems):
        self.nc = nc
        self.sems = sems
        self.cnt = {k: 0 for k in sems}
        self.q = {e: [] for e in ENGS}
        self.lastwait = {e: {k: 0 for k in sems} for e in ENGS}
        self.fenced = {k: 0 for k in sems}

    def fence(self, sem):
        own = DMA_OWNER[sem]
        self.fenced[sem] = self.cnt[sem]
        self.wait(own, sem, self.cnt[sem])

    def wait(self, eng, sem, val):
        if val is None or val <= 0:
            return
        assert val <= self.cnt[sem], f"wait {sem}>={val}, emitted {self.cnt[sem]}"
        if sem in DMA_OWNER and eng != DMA_OWNER[sem]:
            assert val <= self.fenced[sem], \
                f"cross-engine wait {sem}>={val} not fenced (fenced={self.fenced[sem]})"
        if self.lastwait[eng][sem] >= val:
            return
        self.lastwait[eng][sem] = val
        h = self.sems[sem]
        w = lambda e, h=h, v=val: e.wait_ge(h, v)
        w.phase = self.phase
        self.q[eng].append(w)

    phase = "P?"

    def op(self, eng, fn, inc=None, n=1):
        fn.phase = self.phase
        if inc is None:
            self.q[eng].append(fn)
            return None
        self.cnt[inc] += n
        h = self.sems[inc]
        if inc in ("sA", "sV"):
            self.q[eng].append(fn)
            d = lambda e, h=h, n=n: e.drain().then_inc(h, n)
            d.phase = self.phase
            self.q[eng].append(d)
        else:
            d = lambda e, fn=fn, h=h, n=n: fn(e).then_inc(h, n)
            d.phase = self.phase
            self.q[eng].append(d)
        return self.cnt[inc]

    def dma(self, eng, out, in_, inc="dS", **kw):
        m = self.op(eng, lambda e: e.dma_start(out, in_, **kw), inc=inc, n=16)
        if SIM_FENCE_ALL:
            self.fence(inc)
        return m

    def barrier_into(self, eng):
        for s in ("sT", "sA", "sV", "sP"):
            self.wait(eng, s, self.cnt[s])

    def phase_entry(self):
        self.fence("dS")
        self.fence("dG")
        for e in ENGS:
            self.barrier_into(e)
            self.wait(e, "dS", self.fenced["dS"])
            self.wait(e, "dG", self.fenced["dG"])

    def run(self, block):
        def make(q):
            def go(e):
                i = 0
                while i < len(q):
                    ph = getattr(q[i], "phase", "P?")
                    j = i
                    with self.nc.named_scope(ph):
                        while j < len(q) and getattr(q[j], "phase", "P?") == ph:
                            q[j](e)
                            j += 1
                    i = j
            return go
        for name in ENGS:
            getattr(block, name)(make(self.q[name]))


def build_nc():
    nc = bass.Bass("TRN2", target_bir_lowering=False,
                   dynamic_dma_scratch_size=16384)

    xd = nc.dram_tensor("x", [T, D], F32, kind="ExternalInput").ap()
    wq_in = nc.dram_tensor("wq", [D, D], F32, kind="ExternalInput").ap()
    wk_in = nc.dram_tensor("wk", [D, D], F32, kind="ExternalInput").ap()
    wv_in = nc.dram_tensor("wv", [D, D], F32, kind="ExternalInput").ap()
    wo_in = nc.dram_tensor("wo", [D, D], F32, kind="ExternalInput").ap()
    gw_in = nc.dram_tensor("gate_w", [1, D], F32, kind="ExternalInput").ap()
    gb_in = nc.dram_tensor("gate_b", [1, 1], F32, kind="ExternalInput").ap()
    mem_in = nc.dram_tensor("mem_bank", [M, D], F32, kind="ExternalInput").ap()
    wf_in = nc.dram_tensor("ffn_w", [D, D], F32, kind="ExternalInput").ap()
    fs_in = nc.dram_tensor("ffn_scale", [1, D], F32, kind="ExternalInput").ap()
    yd = nc.dram_tensor("y", [T, D], F32, kind="ExternalOutput").ap()

    def scratch(name, shape, dtype):
        kind = "ExternalOutput" if name in TAPS else "Internal"
        return nc.dram_tensor(name, shape, dtype, kind=kind).ap()

    wq_bf = scratch("wq_bf", [D, D], BF16)
    wk_bf = scratch("wk_bf", [D, D], BF16)
    wv_bf = scratch("wv_bf", [D, D], BF16)
    wo_bf = scratch("wo_bf", [D, D], BF16)
    wf_bf = scratch("wf_bf", [D, D], BF16)
    wfs_bf = scratch("wfs_bf", [D, D], BF16)
    mem_bf = scratch("mem_bf", [M, D], BF16)
    x1d = scratch("x1d", [T, D], F32)
    x2d = scratch("x2d", [T, D], F32)
    h1d = scratch("h1d", [T, D], BF16)
    h2d = scratch("h2d", [T, D], BF16)
    h3d = scratch("h3d", [T, D], BF16)
    qfd = scratch("qfd", [D, T], BF16)
    kfd = scratch("kfd", [D, T], BF16)
    vtd = scratch("vtd", [T, D], BF16)
    zfd = scratch("zfd", [D, T], BF16)
    ctxd = scratch("ctxd", [D, T], BF16)
    dlocd = scratch("dlocd", [NH, T], F32)
    wspd = scratch("wspd", [T, M], BF16)
    ufd = scratch("ufd", [D, T], BF16)
    pred = scratch("pred", [D, T], BF16)
    s2dbg = scratch("s2dbg", [T, D], F32)
    smdbg = scratch("smdbg", [8 * 128, 256], F32)
    dend = scratch("dend", [T, 1], F32)
    gd = scratch("gd", [T, 1], F32)

    yd_marker = [yd]
    with ExitStack() as st:
        E = st.enter_context
        ABF = E(nc.sbuf_tensor("ABF", [128, 73728], BF16))
        AFP = E(nc.sbuf_tensor("AFP", [128, 12288], F32))
        cpool = E(nc.sbuf_tensor("cpool", [128, 8], F32))
        masks = E(nc.sbuf_tensor("masks", [128, 6 * 512], BF16))
        ones_c = E(nc.sbuf_tensor("ones_c", [128, 1], BF16))
        ones_rf = E(nc.sbuf_tensor("ones_rf", [1, 128], F32))
        gw2 = E(nc.sbuf_tensor("gw2", [128, 16], BF16))
        gcol = E(nc.sbuf_tensor("gcol", [128, 16], BF16))
        negb = E(nc.sbuf_tensor("negb", [1, 1], F32))
        gbsb = E(nc.sbuf_tensor("gbsb", [1, 1], F32))
        small = E(nc.sbuf_tensor("small", [128, 256], F32))
        PS = [E(nc.psum_tensor(f"PS{i}", [128, 512], F32)) for i in range(8)]

        CONSTS = {RS128: 0, RSD: 1, ESHIFT: 2, -1.0: 3, 1.0: 4}
        for v, i in CONSTS.items():
            nc.const_aps.register(F32, float(v), cpool[:, i:i + 1])

        sems = {}
        for name in ("dS", "dG", "sT", "sA", "sV", "sP"):
            sems[name] = E(nc.semaphore(name))
        p = Prog(nc, sems)

        (b_s1, b_q, b_mean, b_var, b_var2, b_rstd, b_negmr) = (0, 8, 16, 24, 32, 40, 48)
        b_nsq = 64
        (b_S, b_v, b_tk, b_f1, b_f2, b_r1, b_fac, b_on, b_orc, b_clip, b_cc) = (
            96, 104, 112, 120, 128, 136, 144, 152, 160, 168, 176)
        c_t8m, c_negm, c_dc = 184, 192, 196

        def sc(c, k=1):
            return small[:, c:c + k]

        def bc(base, P, k=4):
            return small[:, base + P * 4: base + P * 4 + k]

        ready = {}

        def emit_cast(key, dst, srct):
            ready[key] = p.dma("gpsimd", dst[:, :], srct[:, :], "dG")
            p.fence("dG")
        last_mH = [0]
        last_mV = [0]

        lnguard = {}

        def emit_ln_chain(P, guard_key):
            p.op("vector", lambda e: e.tensor_scalar(
                bc(b_mean, P), bc(b_s1, P), 1.0 / D, None, ALU.mult), inc="sV")
            p.op("vector", lambda e: e.tensor_scalar(
                bc(b_q, P), bc(b_q, P), 1.0 / D, None, ALU.mult), inc="sV")
            p.op("vector", lambda e: e.tensor_tensor(
                bc(b_var, P), bc(b_mean, P), bc(b_mean, P), ALU.mult), inc="sV")
            mv2 = p.op("vector", lambda e: e.scalar_tensor_tensor(
                bc(b_var2, P), bc(b_q, P), EPS_LN, bc(b_var, P),
                op0=ALU.add, op1=ALU.subtract), inc="sV")
            p.wait("vector", "sA", lnguard.get(("hnorm", P), 0))
            p.wait("scalar", "sV", mv2)
            mL1 = p.op("scalar", lambda e: e.activation(
                bc(b_rstd, P), bc(b_var2, P), AF.Ln), inc="sA")
            p.wait("vector", "sA", mL1)
            mNg = p.op("vector", lambda e: e.tensor_scalar(
                bc(b_rstd, P), bc(b_rstd, P), -0.5, None, ALU.mult), inc="sV")
            p.wait("scalar", "sV", mNg)
            mR = p.op("scalar", lambda e: e.activation(
                bc(b_rstd, P), bc(b_rstd, P), AF.Exp), inc="sA")
            p.wait("vector", "sA", mR)
            mNM = p.op("vector", lambda e: e.scalar_tensor_tensor(
                bc(b_negmr, P), bc(b_mean, P), -1.0, bc(b_rstd, P),
                op0=ALU.mult, op1=ALU.mult), inc="sV")
            lnguard[("var2", P)] = mv2
            return mNM

        def emit_factor_chain(P):
            nview = small[:, b_nsq + P * 16: b_nsq + P * 16 + 16].rearrange(
                "p (s k) -> p s k", k=4)
            mNr = p.op("vector", lambda e: e.reduce_sum(
                bc(b_S, P), nview, AXX), inc="sV")
            p.wait("scalar", "sV", mNr)
            mLn = p.op("scalar", lambda e: e.activation(
                bc(b_v, P), bc(b_S, P), AF.Ln), inc="sA")
            p.wait("vector", "sA", mLn)
            mHf = p.op("vector", lambda e: e.tensor_scalar(
                bc(b_v, P), bc(b_v, P), 0.5, None, ALU.mult), inc="sV")
            p.wait("scalar", "sV", mHf)
            mSq = p.op("scalar", lambda e: e.activation(
                bc(b_v, P), bc(b_v, P), AF.Exp), inc="sA")
            p.wait("vector", "sA", mSq)
            p.op("vector", lambda e: e.tensor_scalar(
                bc(b_v, P), bc(b_v, P), EPS_FFN, None, ALU.max), inc="sV")
            mT = p.op("vector", lambda e: e.tensor_scalar(
                bc(b_tk, P), bc(b_v, P), -2.0, None, ALU.mult), inc="sV")
            p.wait("scalar", "sV", mT)
            mEx = p.op("scalar", lambda e: e.activation(
                bc(b_tk, P), bc(b_tk, P), AF.Exp), inc="sA")
            p.wait("vector", "sA", mEx)
            p.op("vector", lambda e: e.tensor_scalar(
                bc(b_f1, P), bc(b_tk, P), -1.0, 1.0, ALU.mult, ALU.add), inc="sV")
            p.op("vector", lambda e: e.tensor_scalar(
                bc(b_tk, P), bc(b_tk, P), 1.0, None, ALU.add), inc="sV")
            p.op("vector", lambda e: e.reciprocal(bc(b_tk, P), bc(b_tk, P)), inc="sV")
            p.op("vector", lambda e: e.tensor_tensor(
                bc(b_f2, P), bc(b_f1, P), bc(b_tk, P), ALU.mult), inc="sV")
            p.op("vector", lambda e: e.tensor_scalar(
                bc(b_r1, P), bc(b_v, P), EPS_FFN, None, ALU.add), inc="sV")
            p.op("vector", lambda e: e.reciprocal(bc(b_r1, P), bc(b_r1, P)), inc="sV")
            p.op("vector", lambda e: e.tensor_tensor(
                bc(b_fac, P), bc(b_f2, P), bc(b_r1, P), ALU.mult), inc="sV")
            p.op("vector", lambda e: e.tensor_tensor(
                bc(b_on, P), bc(b_fac, P), bc(b_v, P), ALU.mult), inc="sV")
            p.op("vector", lambda e: e.tensor_scalar(
                bc(b_on, P), bc(b_on, P), EPS_FFN, None, ALU.max), inc="sV")
            p.op("vector", lambda e: e.reciprocal(bc(b_orc, P), bc(b_on, P)), inc="sV")
            p.op("vector", lambda e: e.tensor_scalar(
                bc(b_clip, P), bc(b_orc, P), MAXNR, 1.0, ALU.mult, ALU.min), inc="sV")
            p.op("vector", lambda e: e.tensor_tensor(
                bc(b_cc, P), bc(b_fac, P), bc(b_clip, P), ALU.mult), inc="sV")
            mC = p.op("vector", lambda e: e.tensor_scalar(
                bc(b_cc, P), bc(b_cc, P), 0.5, None, ALU.mult), inc="sV")
            return mC

        def phase_P0():
            for v, i in CONSTS.items():
                p.op("gpsimd", lambda e, i=i, v=v: e.memset(cpool[:, i:i + 1], float(v)))
            p.op("gpsimd", lambda e: e.memset(ones_c[:, :], 1.0))
            p.op("gpsimd", lambda e: e.memset(ones_rf[:, :], 1.0))
            mt = ABF[:, 0:512]
            mms = p.op("gpsimd", lambda e: e.memset(mt[:, :], 1.0), inc="sP")
            p.wait("gpsimd", "sP", mms)
            for c in range(6):
                off = (c - 2) * 128
                m1 = p.op("gpsimd", lambda e, c=c, off=off: e.affine_select(
                    masks[:, c * 512:(c + 1) * 512], mt[:, :], [[1, 512]],
                    compare_op=ALU.is_ge, fill=0.0, base=-off, channel_multiplier=-1),
                    inc="sP")
                p.wait("gpsimd", "sP", m1)
                m2 = p.op("gpsimd", lambda e, c=c, off=off: e.affine_select(
                    masks[:, c * 512:(c + 1) * 512], masks[:, c * 512:(c + 1) * 512],
                    [[-1, 512]], compare_op=ALU.is_ge, fill=0.0,
                    base=255 + off, channel_multiplier=1), inc="sP")
                p.wait("gpsimd", "sP", m2)
            mP0 = p.op("gpsimd", lambda e: e.engine_nop(), inc="sP")
            for eng in ("scalar", "vector", "tensor"):
                p.wait(eng, "sP", mP0)

            gw_f = AFP[0:1, 0:D]
            fs_row = AFP[0:1, D:2 * D]
            mGW = p.dma("gpsimd", gw_f, gw_in[:, :], "dG")
            p.fence("dG")
            mGB = p.dma("gpsimd", gbsb[0:1, 0:1], gb_in[:, :], "dG")
            p.fence("dG")
            mFS = p.dma("gpsimd", fs_row, fs_in[:, :], "dG")
            p.fence("dG")
            ready["wf"] = p.dma("gpsimd", wf_bf[:, :], wf_in[:, :], "dG")
            p.fence("dG")
            for dst, src, key in ((wo_bf, wo_in, "wo"),):
                ready[key] = p.dma("gpsimd", dst[:, :], src[:, :], "dG")
                p.fence("dG")

            p.wait("vector", "dG", mGB)
            p.op("vector", lambda e: e.tensor_scalar(
                negb[0:1, 0:1], gbsb[0:1, 0:1], -1.0, None, ALU.mult), inc="sV")

            mSCLB = p.cnt["sV"]

            id11 = cpool[0:1, CONSTS[1.0]:CONSTS[1.0] + 1]
            p.wait("tensor", "dG", mGW)
            p.wait("tensor", "sV", mSCLB)
            gev = {}
            for i in range(NC):
                bank = 4 + (i % 2)
                if i >= 2:
                    p.wait("tensor", "sV", gev[i - 2])
                mm = p.op("tensor", lambda e, i=i, bank=bank: e.transpose(
                    PS[bank][:, 0:1], gw_f[0:1, i * 128:(i + 1) * 128], id11), inc="sT")
                p.wait("vector", "sT", mm)
                gev[i] = p.op("vector", lambda e, i=i, bank=bank: e.tensor_copy(
                    gcol[:, i:i + 1], PS[bank][:, 0:1]), inc="sV")
            mGCOL = p.cnt["sV"]

            wob = [ABF[:, 2048 + i * 2048: 2048 + (i + 1) * 2048] for i in range(2)]
            p.fence("dG")
            p.wait("sync", "dG", ready["wo"])
            p.wait("tensor", "sV", mGCOL)
            lastmm = {}
            for half in range(2):
                if half == 1:
                    p.wait("tensor", "sV", p.cnt["sV"])
                for i in range(NC):
                    buf = wob[i % 2]
                    g = half * NC + i
                    if g >= 2:
                        p.wait("sync", "sT", lastmm[g - 2])
                    mL = p.dma("sync", buf, wo_bf[i * 128:(i + 1) * 128, :], "dS")
                    p.fence("dS")
                    p.wait("tensor", "dS", mL)
                    for j in range(8):
                        jj = half * 8 + j
                        mm = p.op("tensor", lambda e, buf=buf, jj=jj, j=j, i=i: e.matmul(
                            PS[j][:, 0:1], buf[:, jj * 128:(jj + 1) * 128],
                            gcol[:, i:i + 1],
                            start=(i == 0), stop=(i == NC - 1)), inc="sT")
                    lastmm[g] = mm
                for j in range(8):
                    jj = half * 8 + j
                    p.wait("vector", "sT", lastmm[half * NC + NC - 1])
                    p.op("vector", lambda e, j=j, jj=jj: e.tensor_copy(
                        gw2[:, jj:jj + 1], PS[j][:, 0:1]), inc="sV")
            ready["gw2"] = p.cnt["sV"]

            fscol = gcol
            fscol = ABF[:, 0:16].bitcast(F32) if False else None
            p.wait("tensor", "dG", mFS)
            p.wait("tensor", "sV", ready["gw2"])
            fev = {}
            for i in range(NC):
                bank = 4 + (i % 2)
                if i >= 2:
                    p.wait("tensor", "sV", fev[i - 2])
                mm = p.op("tensor", lambda e, i=i, bank=bank: e.transpose(
                    PS[bank][:, 0:1], fs_row[0:1, i * 128:(i + 1) * 128], id11), inc="sT")
                p.wait("vector", "sT", mm)
                fev[i] = p.op("vector", lambda e, i=i, bank=bank: e.tensor_copy(
                    small[:, 200 + i: 201 + i], PS[bank][:, 0:1]), inc="sV")
            ready["afp_free"] = p.cnt["sT"]
            p.wait("sync", "dG", ready["wf"])
            p.wait("sync", "sT", p.cnt["sT"])
            wsb2 = [ABF[:, 2048 + i * 2048: 2048 + (i + 1) * 2048] for i in range(2)]
            wst_done = {}
            for i in range(NC):
                buf = wsb2[i % 2]
                if i >= 2:
                    p.wait("sync", "dG", wst_done[i - 2])
                mL = p.dma("sync", buf, wf_bf[:, i * 128:(i + 1) * 128], "dS",
                           transpose=True)
                p.fence("dS")
                p.wait("vector", "dS", mL)
                p.wait("vector", "sV", fev[NC - 1])
                mS = p.op("vector", lambda e, buf=buf, i=i: e.tensor_scalar(
                    buf, buf, small[:, 200 + i: 201 + i], None, ALU.mult), inc="sV")
                p.wait("gpsimd", "sV", mS)
                wst_done[i] = p.dma("gpsimd", wfs_bf[i * 128:(i + 1) * 128, :], buf, "dG")
                p.fence("dG")
            ready["wfs"] = p.cnt["dG"]

        def phase_FFN(src, dst, hdram, wkey, emit_h, entry=True):
            Wf = ABF[:, 0:32768]
            htm = ABF[:, 32768:34816]
            hfm = [ABF[:, 34816 + i * 8192: 34816 + (i + 1) * 8192] for i in range(2)]
            s2b = [ABF[:, 51200 + i * 8192: 51200 + (i + 1) * 8192] for i in range(2)]
            sqb = ABF[:, 67584:69632]
            ob = ABF[:, 69632:73728].bitcast(F32)
            xln = [AFP[:, i * 2048:(i + 1) * 2048] for i in range(4)]
            xr = [AFP[:, 8192 + i * 2048: 8192 + (i + 1) * 2048] for i in range(2)]

            if entry:
                p.phase_entry()
            lnguard.clear()

            hist = {}
            mXd = {}

            def load_xln(g):
                if g >= NT512 * 4:
                    return
                if not entry and g < 4:
                    p.wait("sync", "sT", ready.get("afp_free", 0))
                if g >= 4:
                    p.wait("sync", "sA", hist[("hn", g - 4)])
                mXd[g] = p.dma("sync", xln[g % 4],
                               src[g * 128:(g + 1) * 128, :], "dS")
                p.fence("dS")

            def load_xr(g):
                if g >= NT512 * 4:
                    return
                if g >= 2:
                    p.wait("gpsimd", "sV", hist[("resid", g - 2)])
                mXd[("r", g)] = p.dma("gpsimd", xr[g % 2],
                                      src[g * 128:(g + 1) * 128, :], "dG")
                p.fence("dG")

            def emF1(t):
                P = t % 2
                if t >= 2:
                    p.wait("scalar", "sV", lnguard.get(("var2", P), 0))
                for sub in range(4):
                    g = t * 4 + sub
                    p.wait("vector", "dS", mXd[g])
                    p.wait("scalar", "dS", mXd[g])
                    p.op("vector", lambda e, g=g, sub=sub, P=P: e.reduce_sum(
                        small[:, b_s1 + P * 4 + sub: b_s1 + P * 4 + sub + 1],
                        xln[g % 4], AXX), inc="sV")
                    if sub >= 1:
                        p.wait("scalar", "sV", hist[(t, "qred", sub - 1)])
                    elif t >= 1:
                        p.wait("scalar", "sV", hist[(t - 1, "qred", 3)])
                        p.wait("scalar", "dG", hist[(t - 1, "hst", 3)])
                    mSq = p.op("scalar", lambda e, g=g: e.activation(
                        htm, xln[g % 4], AF.Square), inc="sA")
                    p.wait("vector", "sA", mSq)
                    hist[(t, "qred", sub)] = p.op("vector", lambda e, sub=sub, P=P: e.reduce_sum(
                        small[:, b_q + P * 4 + sub: b_q + P * 4 + sub + 1],
                        htm, AXX), inc="sV")
                mNM = emit_ln_chain(P, None)
                p.wait("scalar", "sV", mNM)
                mHl = None
                for sub in range(4):
                    g = t * 4 + sub
                    if sub >= 1:
                        p.wait("scalar", "dG", hist[(t, "hst", sub - 1)])
                    elif t >= 1:
                        p.wait("scalar", "dG", hist[(t - 1, "hst", 3)])
                    mHl = p.op("scalar", lambda e, g=g, sub=sub, P=P: e.activation(
                        htm, xln[g % 4], AF.Identity,
                        bias=small[:, b_negmr + P * 4 + sub: b_negmr + P * 4 + sub + 1],
                        scale=small[:, b_rstd + P * 4 + sub: b_rstd + P * 4 + sub + 1]),
                        inc="sA")
                    hist[("hn", g)] = mHl
                    p.wait("gpsimd", "sA", mHl)
                    hist[(t, "hst", sub)] = p.dma(
                        "gpsimd", hdram[g * 128:(g + 1) * 128, :], htm, "dG")
                    p.fence("dG")
                lnguard[("hnorm", P)] = mHl
                for sub in range(4):
                    load_xln((t + 1) * 4 + sub)

            def emF2(t):
                P = t % 2
                hf = hfm[P]
                p.wait("sync", "dG", hist[(t, "hst", 3)])
                if t >= 2:
                    p.wait("sync", "sT", hist[(t - 2, "mml")])
                mHrd = 0
                for i in range(NC):
                    mHrd = p.dma("sync", hf[:, i * 512:(i + 1) * 512],
                                 hdram[t * 512:(t + 1) * 512, i * 128:(i + 1) * 128],
                                 "dS", transpose=True)
                p.fence("dS")
                if t == 0:
                    load_xr(0)
                    load_xr(1)
                p.wait("tensor", "dS", mW)
                p.wait("tensor", "dS", mHrd)
                s2 = s2b[P]
                if t >= 2:
                    p.wait("scalar", "sV", hist[("resid", (t - 2) * 4 + 3)])
                mm = None
                for gi in range(16):
                    sub, kb = gi // 4, gi % 4
                    bank = gi % 4
                    if t * 16 + gi >= 4:
                        p.wait("tensor", "sA", hist.get((t, "ev", gi - 4),
                                                        hist.get((t - 1, "ev", gi + 12), 0)))
                    for n in range(NC):
                        mm = p.op("tensor", lambda e, sub=sub, kb=kb, n=n, hf=hf, bank=bank: e.matmul(
                            PS[bank][:, :],
                            hf[:, n * 512 + sub * 128: n * 512 + (sub + 1) * 128],
                            Wf[:, n * 2048 + kb * 512: n * 2048 + (kb + 1) * 512],
                            start=(n == 0), stop=(n == NC - 1)), inc="sT")
                    hist[(t, "mm", gi)] = mm
                    p.wait("scalar", "sT", mm)
                    mE = p.op("scalar", lambda e, gi=gi, s2=s2, bank=bank: e.activation(
                        s2[:, gi * 512:(gi + 1) * 512], PS[bank][:, :], AF.Copy), inc="sA")
                    hist[(t, "ev", gi)] = mE
                    if gi >= 1:
                        p.wait("scalar", "sV", hist[(t, "nqr", gi - 1)])
                    elif t >= 1:
                        p.wait("scalar", "sV", hist[(t - 1, "nqr", 15)])
                    mSq2 = p.op("scalar", lambda e, gi=gi, kb=kb, s2=s2: e.activation(
                        sqb[:, kb * 512:(kb + 1) * 512],
                        s2[:, gi * 512:(gi + 1) * 512], AF.Square), inc="sA")
                    p.wait("vector", "sA", mSq2)
                    hist[(t, "nqr", gi)] = p.op("vector", lambda e, gi=gi, kb=kb, P=P: e.reduce_sum(
                        small[:, b_nsq + P * 16 + gi: b_nsq + P * 16 + gi + 1],
                        sqb[:, kb * 512:(kb + 1) * 512], AXX), inc="sV")
                hist[(t, "mml")] = mm
                mC = emit_factor_chain(P)
                for sub in range(4):
                    g = t * 4 + sub
                    if g >= 1:
                        prev = hist[(t, "ost", sub - 1)] if sub >= 1 else hist[(t - 1, "ost", 3)]
                        p.wait("vector", "dG", prev)
                    p.wait("vector", "dG", mXd[("r", g)])
                    mRe = None
                    for kb in range(4):
                        gi = sub * 4 + kb
                        mRe = p.op("vector", lambda e, g=g, gi=gi, kb=kb, sub=sub, s2=s2, P=P:
                                   e.scalar_tensor_tensor(
                                       ob[:, kb * 512:(kb + 1) * 512],
                                       s2[:, gi * 512:(gi + 1) * 512],
                                       small[:, b_cc + P * 4 + sub: b_cc + P * 4 + sub + 1],
                                       xr[g % 2][:, kb * 512:(kb + 1) * 512],
                                       op0=ALU.mult, op1=ALU.add), inc="sV")
                    hist[("resid", g)] = mRe
                    p.wait("gpsimd", "sV", mRe)
                    hist[(t, "ost", sub)] = p.dma(
                        "gpsimd", dst[g * 128:(g + 1) * 128, :], ob, "dG")
                    p.fence("dG")
                    load_xr(g + 2)

            if not entry:
                for gg in range(4):
                    load_xln(gg)
            p.wait("sync", "dG", ready["wfs"])
            mW = 0
            for i in range(NC):
                mW = p.dma("sync", Wf[:, i * 2048:(i + 1) * 2048],
                           wfs_bf[:, i * 128:(i + 1) * 128], "dS", transpose=True)
            p.fence("dS")
            if entry:
                for gg in range(4):
                    load_xln(gg)
            for t in range(NT512 + 2):
                if t < NT512:
                    emF1(t)
                if t >= 2:
                    emF2(t - 2)
            ready["p1_mm"] = p.cnt["sT"]
            return p.cnt["dG"]

        def preload_projW(key, wsrc):
            Wt = ABF[:, 0:32768]
            p.wait("sync", "sT", ready["p1_mm"])
            p.wait("sync", "dG", ready[key])
            mW = 0
            for i in range(NC):
                mW = p.dma("sync", Wt[:, i * 2048:(i + 1) * 2048],
                           wsrc[:, i * 128:(i + 1) * 128], "dS", transpose=True)
            p.fence("dS")
            ready[("mW", key)] = mW

        def phase_L2(src, hdst):
            htm = [ABF[:, 55296:57344], ABF[:, 57344:59392]]
            sqb2 = [ABF[:, 67584:69632], ABF[:, 69632:71680]]
            xln = [AFP[:, i * 2048:(i + 1) * 2048] for i in range(4)]

            p.phase_entry()
            lnguard.clear()

            hist = {}
            for t in range(NT512):
                P = t % 2
                mXs = []
                for sub in range(4):
                    ts_ = t * 4 + sub
                    if t >= 1:
                        p.wait("sync", "sA", hist[(t - 1, "hn", sub)])
                        p.wait("sync", "sV", hist[(t - 1, "red", sub)])
                    mXs.append(p.dma("sync", xln[sub],
                                     src[ts_ * 128:(ts_ + 1) * 128, :], "dS"))
                p.fence("dS")
                if t >= 2:
                    p.wait("scalar", "sV", lnguard.get(("var2", P), 0))
                for sub in range(4):
                    p.wait("vector", "dS", mXs[sub])
                    p.wait("scalar", "dS", mXs[sub])
                    p.op("vector", lambda e, sub=sub, P=P: e.reduce_sum(
                        small[:, b_s1 + P * 4 + sub: b_s1 + P * 4 + sub + 1],
                        xln[sub], AXX), inc="sV")
                    sq = sqb2[sub % 2]
                    if sub >= 2:
                        p.wait("scalar", "sV", hist[(t, "qred", sub - 2)])
                    elif t >= 1:
                        p.wait("scalar", "sV", hist[(t - 1, "qred", sub + 2)])
                    mSq = p.op("scalar", lambda e, sub=sub, sq=sq: e.activation(
                        sq, xln[sub], AF.Square), inc="sA")
                    p.wait("vector", "sA", mSq)
                    mqq = p.op("vector", lambda e, sub=sub, sq=sq, P=P: e.reduce_sum(
                        small[:, b_q + P * 4 + sub: b_q + P * 4 + sub + 1],
                        sq, AXX), inc="sV")
                    hist[(t, "qred", sub)] = mqq
                    hist[(t, "red", sub)] = mqq
                p.wait("vector", "sA", p.cnt["sA"])
                mNM = emit_ln_chain(P, None)
                p.wait("scalar", "sV", mNM)
                for sub in range(4):
                    ts_ = t * 4 + sub
                    ht = htm[sub % 2]
                    if sub >= 2:
                        p.wait("scalar", "dG", hist[(t, "hst", sub - 2)])
                    elif t >= 1:
                        p.wait("scalar", "dG", hist[(t - 1, "hst", sub + 2)])
                    mH = p.op("scalar", lambda e, sub=sub, ht=ht, P=P: e.activation(
                        ht, xln[sub], AF.Identity,
                        bias=small[:, b_negmr + P * 4 + sub: b_negmr + P * 4 + sub + 1],
                        scale=small[:, b_rstd + P * 4 + sub: b_rstd + P * 4 + sub + 1]),
                        inc="sA")
                    hist[(t, "hn", sub)] = mH
                    p.wait("gpsimd", "sA", mH)
                    hist[(t, "hst", sub)] = p.dma(
                        "gpsimd", hdst[ts_ * 128:(ts_ + 1) * 128, :], ht, "dG")
                    p.fence("dG")
                lnguard[("hnorm", P)] = hist[(t, "hn", 3)]
                ready[("h2t", t)] = hist[(t, "hst", 3)]
            ready["h2"] = p.cnt["dG"]

        def phase_proj(kind, wsrc, outd, entry=True):
            Wt = ABF[:, 0:32768]
            hfm = [ABF[:, 32768 + i * 8192: 32768 + (i + 1) * 8192] for i in range(2)]
            stg = [ABF[:, 51200 + i * 512: 51200 + (i + 1) * 512] for i in range(4)]

            if entry:
                p.phase_entry()
                mW = 0
                if kind in ("q", "k", "v"):
                    p.wait("sync", "dG", ready[{"q": "wq", "k": "wk", "v": "wv"}[kind]])
                    for i in range(NC):
                        mW = p.dma("sync", Wt[:, i * 2048:(i + 1) * 2048],
                                   wsrc[:, i * 128:(i + 1) * 128], "dS", transpose=True)
                else:
                    p.wait("sync", "dG", ready["wk"])
                    for i in range(NC):
                        mW = p.dma("sync", Wt[:, i * 2048:(i + 1) * 2048],
                                   wsrc[i * 128:(i + 1) * 128, :], "dS")
                p.fence("dS")
                p.wait("sync", "dG", ready["h2"])
            else:
                mW = ready[("mW", {"q": "wq"}[kind])]

            hist = {}
            g = 0
            for t in range(NT512):
                hf = hfm[t % 2]
                if not entry:
                    p.wait("sync", "dG", ready[("h2t", t)])
                if t >= 2:
                    p.wait("sync", "sT", hist[("mml", t - 2)])
                mHrd = 0
                for i in range(NC):
                    mHrd = p.dma("sync", hf[:, i * 512:(i + 1) * 512],
                                 h2d[t * 512:(t + 1) * 512, i * 128:(i + 1) * 128],
                                 "dS", transpose=True)
                p.fence("dS")
                p.wait("tensor", "dS", mW)
                p.wait("tensor", "dS", mHrd)
                if kind in ("q", "k", "z"):
                    for m in range(NC):
                        bank = g % 4
                        if g >= 4:
                            p.wait("tensor", "sA", hist[("ev", g - 4)])
                        mm = None
                        for n in range(NC):
                            mm = p.op("tensor", lambda e, m=m, n=n, hf=hf, bank=bank: e.matmul(
                                PS[bank][:, :],
                                Wt[:, n * 2048 + m * 128: n * 2048 + (m + 1) * 128],
                                hf[:, n * 512:(n + 1) * 512],
                                start=(n == 0), stop=(n == NC - 1)), inc="sT")
                        hist[("mm", g)] = mm
                        p.wait("scalar", "sT", mm)
                        if g >= 4:
                            p.wait("scalar", "dG", hist[("st", g - 4)])
                        sg = stg[g % 4]
                        mE = p.op("scalar", lambda e, sg=sg, bank=bank: e.activation(
                            sg, PS[bank][:, :], AF.Copy), inc="sA")
                        hist[("ev", g)] = mE
                        p.wait("gpsimd", "sA", mE)
                        hist[("st", g)] = p.dma(
                            "gpsimd", outd[m * 128:(m + 1) * 128, t * 512:(t + 1) * 512],
                            sg, "dG")
                        p.fence("dG")
                        g += 1
                else:
                    for sidx in range(4):
                        for obk in range(4):
                            bank = g % 4
                            if g >= 4:
                                p.wait("tensor", "sA", hist[("ev", g - 4)])
                            mm = None
                            for n in range(NC):
                                mm = p.op("tensor", lambda e, sidx=sidx, obk=obk, n=n, hf=hf, bank=bank: e.matmul(
                                    PS[bank][:, :],
                                    hf[:, n * 512 + sidx * 128: n * 512 + (sidx + 1) * 128],
                                    Wt[:, n * 2048 + obk * 512: n * 2048 + (obk + 1) * 512],
                                    start=(n == 0), stop=(n == NC - 1)), inc="sT")
                            hist[("mm", g)] = mm
                            p.wait("scalar", "sT", mm)
                            if g >= 4:
                                p.wait("scalar", "dG", hist[("st", g - 4)])
                            sg = stg[g % 4]
                            mE = p.op("scalar", lambda e, sg=sg, bank=bank: e.activation(
                                sg, PS[bank][:, :], AF.Copy), inc="sA")
                            hist[("ev", g)] = mE
                            p.wait("gpsimd", "sA", mE)
                            hist[("st", g)] = p.dma(
                                "gpsimd",
                                outd[t * 512 + sidx * 128: t * 512 + (sidx + 1) * 128,
                                     obk * 512:(obk + 1) * 512], sg, "dG")
                            p.fence("dG")
                            g += 1
                hist[("mml", t)] = hist[("mm", g - 1)]
            ready[kind + "done"] = p.cnt["dG"]

        def phase_B():
            qbs = [ABF[:, i * 8192:(i + 1) * 8192] for i in range(2)]
            kbs = [ABF[:, 16384 + i * 12288: 16384 + (i + 1) * 12288] for i in range(2)]
            vb = ABF[:, 40960:53248]
            eb = [ABF[:, 53248 + i * 3072: 53248 + (i + 1) * 3072] for i in range(2)]
            wb = [ABF[:, 59392 + i * 3072: 59392 + (i + 1) * 3072] for i in range(2)]
            ctxs = [ABF[:, 65536 + i * 512: 65536 + (i + 1) * 512] for i in range(2)]
            dr_t = [AFP[0:1, 512 + i * 512: 512 + (i + 1) * 512] for i in range(2)]

            p.phase_entry()
            p.wait("sync", "dG", ready["qdone"])
            p.wait("sync", "dG", ready["kdone"])
            p.wait("sync", "dG", ready["vdone"])

            hist = {}
            scg = 0
            for qt in range(NT512):
                base_kb = 4 * qt - 2
                chunks = [c for c in range(6) if 0 <= base_kb + c < NT128]
                qp = qt % 2
                qb, kb = qbs[qp], kbs[qp]
                if qt >= 2:
                    p.wait("sync", "sT", hist[("scl", qt - 2)])
                mL = 0
                for h in range(NH):
                    mL = p.dma("sync", qb[:, h * 512:(h + 1) * 512],
                               qfd[h * 128:(h + 1) * 128, qt * 512:(qt + 1) * 512], "dS")
                for h in range(NH):
                    for c in chunks:
                        ki = base_kb + c
                        mL = p.dma("sync", kb[:, h * 768 + c * 128: h * 768 + (c + 1) * 128],
                                   kfd[h * 128:(h + 1) * 128, ki * 128:(ki + 1) * 128], "dS")
                mQK = mL
                if qt >= 1:
                    p.wait("sync", "sT", hist[("pvl", qt - 1, NH - 1)])
                for c in chunks:
                    ki = base_kb + c
                    mL = p.dma("sync", vb[:, c * 2048:(c + 1) * 2048],
                               vtd[ki * 128:(ki + 1) * 128, :], "dS")
                p.fence("dS")
                p.wait("tensor", "dS", mQK)

                def emA(h):
                    nonlocal scg
                    hp = h % 2
                    key_prev = (qt, h - 2) if h >= 2 else ((qt - 1, h + 14) if qt >= 1 else None)
                    for ci, c in enumerate(chunks):
                        bank = scg % 2
                        if scg >= 2:
                            p.wait("tensor", "sA", hist[("exp", scg - 2)])
                        mm = p.op("tensor", lambda e, h=h, c=c, bank=bank, kb=kb, qb=qb: e.matmul(
                            PS[bank][:, :],
                            kb[:, h * 768 + c * 128: h * 768 + (c + 1) * 128],
                            qb[:, h * 512:(h + 1) * 512], start=True, stop=True), inc="sT")
                        p.wait("scalar", "sT", mm)
                        if key_prev is not None:
                            p.wait("scalar", "sV", hist.get(("wm", *key_prev, c), 0))
                        mex = p.op("scalar", lambda e, c=c, hp=hp, bank=bank: e.activation(
                            eb[hp][:, c * 512:(c + 1) * 512], PS[bank][:, :], AF.Exp,
                            bias=ESHIFT, scale=RS128), inc="sA")
                        hist[("exp", scg)] = mex
                        p.wait("vector", "sA", mex)
                        if key_prev is not None:
                            p.wait("vector", "sT", hist.get(("pvl", *key_prev), 0))
                        mw = p.op("vector", lambda e, c=c, hp=hp: e.tensor_tensor(
                            wb[hp][:, c * 512:(c + 1) * 512],
                            eb[hp][:, c * 512:(c + 1) * 512],
                            masks[:, c * 512:(c + 1) * 512], ALU.mult), inc="sV")
                        hist[("wm", qt, h, c)] = mw
                        hist[("scl", qt)] = mm
                        scg += 1

                def emB(h):
                    hp = h % 2
                    key_prev = (qt, h - 2) if h >= 2 else ((qt - 1, h + 14) if qt >= 1 else None)
                    if h == 0:
                        p.wait("tensor", "dS", mL)
                    if key_prev is not None:
                        p.wait("tensor", "sV", hist.get(("drv", *key_prev), 0))
                        p.wait("tensor", "sA", hist.get(("cev", *key_prev), 0))
                    mde = mpv = None
                    for ci, c in enumerate(chunks):
                        p.wait("tensor", "sV", hist[("wm", qt, h, c)])
                        first, last = ci == 0, ci == len(chunks) - 1
                        mde = p.op("tensor", lambda e, c=c, hp=hp, first=first, last=last: e.matmul(
                            PS[2 + hp][0:1, :], ones_c[:, :],
                            wb[hp][:, c * 512:(c + 1) * 512],
                            start=first, stop=last), inc="sT")
                        mpv = p.op("tensor", lambda e, c=c, h=h, hp=hp, first=first, last=last: e.matmul(
                            PS[5 + hp][:, :],
                            vb[:, c * 2048 + h * 128: c * 2048 + (h + 1) * 128],
                            wb[hp][:, c * 512:(c + 1) * 512],
                            start=first, stop=last), inc="sT")
                    hist[("pvl", qt, h)] = mpv
                    hist[("del", qt, h)] = mde

                def emC(h):
                    hp = h % 2
                    key_prev = (qt, h - 2) if h >= 2 else ((qt - 1, h + 14) if qt >= 1 else None)
                    p.wait("vector", "sT", hist[("del", qt, h)])
                    if key_prev is not None:
                        p.wait("vector", "dG", hist.get(("dst", *key_prev), 0))
                    mdr = p.op("vector", lambda e, hp=hp: e.tensor_copy(
                        dr_t[hp], PS[2 + hp][0:1, :]), inc="sV")
                    hist[("drv", qt, h)] = mdr
                    ctx = ctxs[hp]
                    p.wait("scalar", "sT", hist[("pvl", qt, h)])
                    if key_prev is not None:
                        p.wait("scalar", "dG", hist.get(("cst", *key_prev), 0))
                    mcv = p.op("scalar", lambda e, ctx=ctx, hp=hp: e.activation(
                        ctx, PS[5 + hp][:, :], AF.Copy), inc="sA")
                    hist[("cev", qt, h)] = mcv
                    p.wait("gpsimd", "sA", mcv)
                    hist[("cst", qt, h)] = p.dma(
                        "gpsimd", ctxd[h * 128:(h + 1) * 128, qt * 512:(qt + 1) * 512],
                        ctx, "dG")
                    p.wait("gpsimd", "sV", mdr)
                    hist[("dst", qt, h)] = p.dma(
                        "gpsimd", dlocd[h:h + 1, qt * 512:(qt + 1) * 512],
                        dr_t[hp], "dG")
                    p.fence("dG")

                for k in range(NH + 2):
                    if k < NH:
                        emA(k)
                    if 1 <= k <= NH:
                        emB(k - 1)
                    if k >= 2:
                        emC(k - 2)
            ready["ctx"] = p.cnt["dG"]
            ready["dloc"] = p.cnt["dG"]

        def phase_C1():
            memT = ABF[:, 0:65536]
            zb = [ABF[:, 65536 + i * 2048: 65536 + (i + 1) * 2048] for i in range(2)]
            ew = ABF[:, 69632:73728]
            wsbs = [AFP[:, 8192:9216].bitcast(BF16), AFP[:, 9216:10240].bitcast(BF16)]
            sims = [AFP[:, i * 4096:(i + 1) * 4096] for i in range(2)]

            p.phase_entry()
            p.wait("sync", "dG", ready["mem"])
            mMT = 0
            for i in range(NC):
                mMT = p.dma("sync", memT[:, i * 4096:(i + 1) * 4096],
                            mem_bf[:, i * 128:(i + 1) * 128], "dS", transpose=True)
            p.fence("dS")
            p.wait("sync", "dG", ready["zdone"])

            hist = {}
            for t in range(NT128):
                zt = zb[t % 2]
                if t >= 2:
                    p.wait("sync", "sT", hist[("mml", t - 2)])
                mZ = 0
                for i in range(NC):
                    mZ = p.dma("sync", zt[:, i * 128:(i + 1) * 128],
                               zfd[i * 128:(i + 1) * 128, t * 128:(t + 1) * 128], "dS")
                p.fence("dS")
                p.wait("tensor", "dS", mMT)
                p.wait("tensor", "dS", mZ)
                mm = None
                for mbl in range(8):
                    if t >= 1:
                        p.wait("tensor", "sA", hist[("sev", t - 1, mbl)])
                    for n in range(NC):
                        mm = p.op("tensor", lambda e, zt=zt, n=n, mbl=mbl: e.matmul(
                            PS[mbl][:, :], zt[:, n * 128:(n + 1) * 128],
                            memT[:, n * 4096 + mbl * 512: n * 4096 + (mbl + 1) * 512],
                            start=(n == 0), stop=(n == NC - 1)), inc="sT")
                    hist[("smm", t, mbl)] = mm
                hist[("mml", t)] = mm
                sim = sims[t % 2]
                if t >= 2:
                    p.wait("scalar", "sV", hist[("wsp", t - 2)])
                for mbl in range(8):
                    p.wait("scalar", "sT", hist[("smm", t, mbl)])
                    mE = p.op("scalar", lambda e, sim=sim, mbl=mbl: e.activation(
                        sim[:, mbl * 512:(mbl + 1) * 512], PS[mbl][:, :], AF.Copy,
                        bias=0.0, scale=RSD), inc="sA")
                    hist[("sev", t, mbl)] = mE
                p.wait("vector", "sA", hist[("sev", t, 7)])
                if t >= 1:
                    p.wait("vector", "sA", hist[("ex", t - 1)])
                p.op("vector", lambda e, sim=sim: e.max(out=sc(c_t8m, 8), in_=sim))
                p.op("vector", lambda e: e.drain())
                mNM = p.op("vector", lambda e: e.tensor_scalar(
                    sc(c_negm), small[:, c_t8m:c_t8m + 1], -1.0, None, ALU.mult), inc="sV")
                p.wait("scalar", "sV", mNM)
                if t >= 1:
                    p.wait("scalar", "sV", hist[("wsp", t - 1)])
                    p.wait("scalar", "dG", hist[("wst", t - 1)])
                mEx = p.op("scalar", lambda e, sim=sim: e.activation(
                    ew, sim, AF.Exp, bias=sc(c_negm), scale=1.0), inc="sA")
                hist[("ex", t)] = mEx
                p.wait("vector", "sA", mEx)
                if t >= 1:
                    p.wait("vector", "dG", hist[("wst", t - 1)])
                if t >= 1:
                    p.wait("vector", "dG", hist[("dst", t - 1)])
                mW = None
                for hh in range(2):
                    wsb = wsbs[hh]
                    if t >= 1:
                        p.wait("vector", "dG", hist[("wst" + str(hh), t - 1)])
                    mWh = p.op("vector", lambda e, sim=sim, hh=hh, wsb=wsb: e.scalar_tensor_tensor(
                        wsb, sim[:, hh * 2048:(hh + 1) * 2048],
                        small[:, c_t8m + 7:c_t8m + 8], ew[:, hh * 2048:(hh + 1) * 2048],
                        op0=ALU.is_ge, op1=ALU.mult), inc="sV")
                    mDh = p.op("vector", lambda e, hh=hh, wsb=wsb: e.reduce_sum(
                        small[:, c_dc + hh: c_dc + hh + 1], wsb, AXX), inc="sV")
                    p.wait("gpsimd", "sV", mDh)
                    mSt = p.dma("gpsimd",
                                wspd[t * 128:(t + 1) * 128, hh * 2048:(hh + 1) * 2048],
                                wsb, "dG")
                    p.fence("dG")
                    hist[("wst" + str(hh), t)] = mSt
                    mW = mDh
                hist[("wsp", t)] = mW
                mDC = p.op("vector", lambda e: e.tensor_tensor(
                    small[:, c_dc + 2: c_dc + 3], sc(c_dc),
                    small[:, c_dc + 1: c_dc + 2], ALU.add), inc="sV")
                p.wait("gpsimd", "sV", mDC)
                hist[("wst", t)] = hist[("wst1", t)]
                hist[("dst", t)] = p.dma("gpsimd", dend[t * 128:(t + 1) * 128, 0:1],
                                         small[:, c_dc + 2: c_dc + 3], "dG")
                p.fence("dG")
            ready["wsp"] = p.cnt["dG"]
            ready["den"] = p.cnt["dG"]

        def phase_C2():
            memS = ABF[:, 0:65536]
            wtb = [ABF[:, 65536 + i * 512: 65536 + (i + 1) * 512] for i in range(8)]
            ub = ABF[:, 69632:73728]

            p.phase_entry()
            p.wait("sync", "dG", ready["mem"])
            mMS = 0
            for i in range(NM):
                mMS = p.dma("sync", memS[:, i * 2048:(i + 1) * 2048],
                            mem_bf[i * 128:(i + 1) * 128, :], "dS")
            p.fence("dS")
            p.wait("sync", "dG", ready["wsp"])

            hist = {}
            G = 0
            for t in range(NT512):
                for half in range(2):
                    hidx = t * 2 + half
                    for mg in range(NM // 4):
                        Gg = hidx * (NM // 4) + mg
                        if Gg >= 2:
                            p.wait("sync", "sT", hist[("wmm", Gg - 2)])
                        mWT = 0
                        for mi in range(4):
                            m = mg * 4 + mi
                            wt = wtb[(m % 8)]
                            mWT = p.dma("sync", wt,
                                        wspd[t * 512:(t + 1) * 512, m * 128:(m + 1) * 128],
                                        "dS", transpose=True)
                        p.fence("dS")
                        p.wait("tensor", "dS", mMS)
                        p.wait("tensor", "dS", mWT)
                        if mg == 0 and hidx >= 1:
                            for j in range(8):
                                p.wait("tensor", "sA", hist[("uev", hidx - 1, j)])
                        mm = None
                        for mi in range(4):
                            m = mg * 4 + mi
                            wt = wtb[(m % 8)]
                            for j in range(8):
                                dj = half * 8 + j
                                mm = p.op("tensor", lambda e, wt=wt, m=m, dj=dj, j=j: e.matmul(
                                    PS[j][:, :],
                                    memS[:, m * 2048 + dj * 128: m * 2048 + (dj + 1) * 128],
                                    wt, start=(m == 0), stop=(m == NM - 1)), inc="sT")
                        hist[("wmm", Gg)] = mm
                        G += 1
                    for j in range(8):
                        dj = half * 8 + j
                        p.wait("scalar", "sT", hist[("wmm", hidx * (NM // 4) + NM // 4 - 1)])
                        if hidx >= 1:
                            p.wait("scalar", "dG", hist[("ust", hidx - 1, j)])
                        mE = p.op("scalar", lambda e, j=j: e.activation(
                            ub[:, j * 512:(j + 1) * 512], PS[j][:, :], AF.Copy), inc="sA")
                        hist[("uev", hidx, j)] = mE
                        p.wait("gpsimd", "sA", mE)
                        hist[("ust", hidx, j)] = p.dma(
                            "gpsimd", ufd[dj * 128:(dj + 1) * 128, t * 512:(t + 1) * 512],
                            ub[:, j * 512:(j + 1) * 512], "dG")
                    p.fence("dG")
            ready["u"] = p.cnt["dG"]

        def phase_D1():
            WvT = ABF[:, 0:32768]
            ubfs = [ABF[:, 32768 + i * 8192: 32768 + (i + 1) * 8192] for i in range(2)]
            cbf = ABF[:, 49152:57344]
            cdvs = [ABF[:, 57344 + i * 8192: 57344 + (i + 1) * 8192] for i in range(2)]
            prv = AFP[:, 9728:10240].bitcast(BF16)
            prb = [prv[:, 0:512], prv[:, 512:1024]]
            ptmp = AFP[:, 10240:10496].bitcast(BF16)
            grv = AFP[:, 10496:11008].bitcast(BF16)
            grbs = [grv[:, 0:512], grv[:, 512:1024]]
            dinv = AFP[0:1, 0:8192]
            dr_t = AFP[0:1, 8192:8704]
            g_t = AFP[0:1, 8704:9216]
            gr_t = AFP[0:1, 9216:9728]

            p.phase_entry()
            p.wait("sync", "dG", ready["wv"])
            mWv = 0
            for i in range(NC):
                mWv = p.dma("sync", WvT[:, i * 2048:(i + 1) * 2048],
                            wv_bf[:, i * 128:(i + 1) * 128], "dS", transpose=True)
            p.fence("dS")
            p.wait("sync", "dG", ready["u"])
            p.wait("sync", "dG", ready["ctx"])
            p.wait("sync", "dG", ready["dloc"])
            p.wait("sync", "dG", ready["den"])

            hist = {}
            mU = {}
            for t in range(NT512 + 1):
                if t < NT512:
                    if t >= 1:
                        p.wait("sync", "sT", hist[("dlbl", t - 1)])
                    mDL = 0
                    for h in range(NH):
                        mDL = p.dma("sync", dinv[0:1, h * 512:(h + 1) * 512],
                                    dlocd[h:h + 1, t * 512:(t + 1) * 512], "dS")
                    if t >= 1:
                        p.wait("sync", "sV", hist[("grtt", t - 1)])
                    mDR = p.dma("sync", dr_t, dend[t * 512:(t + 1) * 512, 0:1].rearrange("a b -> b a"), "dS")
                    p.fence("dS")
                    if t >= 2:
                        p.wait("sync", "sT", hist[("amml", t - 2)])
                    mUc = 0
                    for i in range(NC):
                        mUc = p.dma("sync", ubfs[t % 2][:, i * 512:(i + 1) * 512],
                                    ufd[i * 128:(i + 1) * 128, t * 512:(t + 1) * 512], "dS")
                    mU[t] = mUc
                    p.fence("dS")
                    if t >= 1:
                        p.wait("sync", "sV", hist[("cdvl", t - 1)])
                    mC = 0
                    for i in range(NC):
                        mC = p.dma("sync", cbf[:, i * 512:(i + 1) * 512],
                                   ctxd[i * 128:(i + 1) * 128, t * 512:(t + 1) * 512], "dS")
                    p.fence("dS")
                    p.wait("vector", "dS", mDL)
                    mRec = p.op("vector", lambda e: e.reciprocal(dinv, dinv), inc="sV")

                def em_dlb(j):
                    bank = 5 if j % 2 == 0 else 7
                    gidx = t * NC + j
                    if j == 0:
                        p.wait("tensor", "sV", mRec)
                    if gidx >= 2:
                        p.wait("tensor", "sV", hist[("cdv", gidx - 2)])
                    mm = p.op("tensor", lambda e, j=j, bank=bank: e.matmul(
                        PS[bank][:, :], ones_rf[:, :],
                        dinv[0:1, j * 512:(j + 1) * 512],
                        start=True, stop=True), inc="sT")
                    hist[("dlb", gidx)] = mm

                def em_cdv(j, mC=None, mRec=None):
                    bank = 5 if j % 2 == 0 else 7
                    gidx = t * NC + j
                    cdv = cdvs[t % 2]
                    p.wait("vector", "sT", hist[("dlb", gidx)])
                    if j == 0:
                        p.wait("vector", "dS", mC)
                    mm = p.op("vector", lambda e, j=j, bank=bank, cdv=cdv: e.tensor_tensor(
                        cdv[:, j * 512:(j + 1) * 512], PS[bank][:, :],
                        cbf[:, j * 512:(j + 1) * 512], ALU.mult), inc="sV")
                    hist[("cdv", gidx)] = mm

                def em_gm(j):
                    cdv = cdvs[t % 2]
                    p.wait("tensor", "sV", hist[("cdv", t * NC + j)])
                    if j == 0 and t >= 1:
                        p.wait("tensor", "sA", hist[("gex", t - 1)])
                    mm = p.op("tensor", lambda e, j=j, cdv=cdv: e.matmul(
                        PS[6][0:1, :], gw2[:, j:j + 1], cdv[:, j * 512:(j + 1) * 512],
                        start=(j == 0), stop=(j == NC - 1)), inc="sT")
                    hist[("gm", t, j)] = mm

                def em_pass2(tp, j):
                    gidx = tp * NC + j
                    bank = j % 4
                    ubf = ubfs[tp % 2]
                    cdv = cdvs[tp % 2]
                    grb = grbs[tp % 2]
                    if j == 0:
                        p.wait("tensor", "dS", mWv)
                        p.wait("tensor", "dS", mU[tp])
                    if gidx >= 4:
                        p.wait("tensor", "sV", hist[("pre", gidx - 4)])
                    mm = None
                    for dp in range(NC):
                        mm = p.op("tensor", lambda e, dp=dp, j=j, bank=bank, ubf=ubf: e.matmul(
                            PS[bank][:, :],
                            WvT[:, dp * 2048 + j * 128: dp * 2048 + (j + 1) * 128],
                            ubf[:, dp * 512:(dp + 1) * 512],
                            start=(dp == 0), stop=(dp == NC - 1)), inc="sT")
                    hist[("amm", gidx)] = mm
                    pb = prb[gidx % 2]
                    p.wait("vector", "sT", mm)
                    if j == 0:
                        p.wait("vector", "sV", hist[("grbev", tp)])
                    if gidx >= 2:
                        p.wait("vector", "dG", hist[("pst", gidx - 2)])
                    mPt = p.op("vector", lambda e, bank=bank, grb=grb: e.tensor_tensor(
                        ptmp, PS[bank][:, :], grb, ALU.mult), inc="sV")
                    p.wait("vector", "sV", mPt)
                    mPre = p.op("vector", lambda e, pb=pb, j=j, cdv=cdv: e.tensor_tensor(
                        pb, ptmp, cdv[:, j * 512:(j + 1) * 512], ALU.add), inc="sV")
                    hist[("pre", gidx)] = mPre
                    p.wait("gpsimd", "sV", mPre)
                    hist[("pst", gidx)] = p.dma(
                        "gpsimd", pred[j * 128:(j + 1) * 128, tp * 512:(tp + 1) * 512],
                        pb, "dG")
                    p.fence("dG")

                for j in range(NC + 2):
                    if t >= 1 and j < NC:
                        em_pass2(t - 1, j)
                    if t < NT512:
                        if j < NC:
                            em_dlb(j)
                        if 1 <= j <= NC:
                            em_cdv(j - 1, mC, mRec)
                        if j >= 2:
                            em_gm(j - 2)
                if t >= 1:
                    hist[("amml", t - 1)] = hist[("amm", (t - 1) * NC + NC - 1)]

                if t < NT512:
                    hist[("dlbl", t)] = hist[("dlb", t * NC + NC - 1)]
                    hist[("cdvl", t)] = hist[("cdv", t * NC + NC - 1)]
                    hist[("gml", t)] = hist[("gm", t, NC - 1)]
                    p.wait("scalar", "sT", hist[("gml", t)])
                    if t >= 1:
                        p.wait("scalar", "sV", hist[("grtt", t - 1)])
                    mge = p.op("scalar", lambda e: e.activation(
                        g_t, PS[6][0:1, :], AF.Exp, bias=negb[0:1, 0:1], scale=-1.0), inc="sA")
                    hist[("gex", t)] = mge
                    p.wait("vector", "sA", mge)
                    p.op("vector", lambda e: e.tensor_scalar(g_t, g_t, 1.0, None, ALU.add), inc="sV")
                    p.op("vector", lambda e: e.reciprocal(g_t, g_t), inc="sV")
                    p.wait("vector", "dS", mDR)
                    p.op("vector", lambda e: e.reciprocal(dr_t, dr_t), inc="sV")
                    if t >= 1:
                        p.wait("vector", "sT", hist[("bcm", t - 1)])
                    mRow = p.op("vector", lambda e: e.tensor_tensor(
                        gr_t, dr_t, g_t, ALU.mult), inc="sV")
                    hist[("grtt", t)] = mRow
                    p.wait("tensor", "sV", mRow)
                    if t >= 1:
                        p.wait("tensor", "sV", hist[("grbev", t - 1)])
                    mBC = p.op("tensor", lambda e: e.matmul(
                        PS[4][:, :], ones_rf[:, :], gr_t, start=True, stop=True), inc="sT")
                    hist[("bcm", t)] = mBC
                    p.wait("vector", "sT", mBC)
                    mGRB = p.op("vector", lambda e, t=t: e.tensor_copy(
                        grbs[t % 2], PS[4][:, :]), inc="sV")
                    hist[("grbev", t)] = mGRB
            ready["pre"] = p.cnt["dG"]

        def phase_D2():
            WoT = ABF[:, 0:32768]
            pfm = [ABF[:, 32768 + i * 8192: 32768 + (i + 1) * 8192] for i in range(2)]
            xb = [AFP[:, i * 2048:(i + 1) * 2048] for i in range(2)]
            x2sb = [AFP[:, 4096 + i * 2048: 4096 + (i + 1) * 2048] for i in range(2)]

            p.phase_entry()
            p.wait("sync", "dG", ready["wo"])
            mWo = 0
            for i in range(NC):
                mWo = p.dma("sync", WoT[:, i * 2048:(i + 1) * 2048],
                            wo_bf[:, i * 128:(i + 1) * 128], "dS", transpose=True)
            p.fence("dS")
            p.wait("sync", "dG", ready["pre"])
            p.barrier_into("gpsimd")

            hist = {}
            g = 0
            for t in range(NT512):
                pf = pfm[t % 2]
                if t >= 2:
                    p.wait("sync", "sT", hist[("mml", t - 2)])
                mPf = 0
                for i in range(NC):
                    mPf = p.dma("sync", pf[:, i * 512:(i + 1) * 512],
                                pred[i * 128:(i + 1) * 128, t * 512:(t + 1) * 512], "dS")
                p.fence("dS")
                p.wait("tensor", "dS", mWo)
                p.wait("tensor", "dS", mPf)
                for s in range(4):
                    ts_ = t * 4 + s
                    xt = xb[ts_ % 2]
                    if ts_ >= 2:
                        p.wait("gpsimd", "sV", hist[("x2", ts_ - 2)])
                    mX = p.dma("gpsimd", xt, x1d[ts_ * 128:(ts_ + 1) * 128, :], "dG")
                    p.fence("dG")
                    x2t = x2sb[ts_ % 2]
                    mRe = None
                    for obk in range(4):
                        bank = g % 4
                        if g >= 4:
                            p.wait("tensor", "sV", hist[("oev", g - 4)])
                        mm = None
                        for c in range(NC):
                            mm = p.op("tensor", lambda e, s=s, obk=obk, c=c, pf=pf, bank=bank: e.matmul(
                                PS[bank][:, :],
                                pf[:, c * 512 + s * 128: c * 512 + (s + 1) * 128],
                                WoT[:, c * 2048 + obk * 512: c * 2048 + (obk + 1) * 512],
                                start=(c == 0), stop=(c == NC - 1)), inc="sT")
                        hist[("mm", g)] = mm
                        p.wait("vector", "sT", mm)
                        p.wait("vector", "dG", mX)
                        if ts_ >= 2:
                            p.wait("vector", "dG", hist[("x2st", ts_ - 2)])
                        mRe = p.op("vector", lambda e, obk=obk, x2t=x2t, xt=xt, bank=bank:
                                   e.tensor_tensor(
                                       x2t[:, obk * 512:(obk + 1) * 512], PS[bank][:, :],
                                       xt[:, obk * 512:(obk + 1) * 512], ALU.add), inc="sV")
                        hist[("oev", g)] = mRe
                        g += 1
                    hist[("x2", ts_)] = mRe
                    p.wait("gpsimd", "sV", mRe)
                    hist[("x2st", ts_)] = p.dma(
                        "gpsimd", x2d[ts_ * 128:(ts_ + 1) * 128, :], x2t, "dG")
                    p.fence("dG")
                hist[("mml", t)] = hist[("mm", g - 1)]
            ready["x2"] = p.cnt["dG"]

        ph = {
            "P0": phase_P0,
            "P1": lambda: phase_FFN(xd, x1d, h1d, "wf", False, entry=False),
            "L2": lambda: (emit_cast("wq", wq_bf, wq_in),
                           emit_cast("wk", wk_bf, wk_in),
                           preload_projW("wq", wq_bf),
                           phase_L2(x1d, h2d)),
            "Aq": lambda: (emit_cast("wv", wv_bf, wv_in),
                           phase_proj("q", wq_bf, qfd, entry=False)),
            "Ak": lambda: (emit_cast("mem", mem_bf, mem_in),
                           phase_proj("k", wk_bf, kfd)),
            "Av": lambda: phase_proj("v", wv_bf, vtd),
            "Az": lambda: phase_proj("z", wk_bf, zfd),
            "B": phase_B,
            "C1": phase_C1,
            "C2": phase_C2,
            "D1": phase_D1,
            "D2": phase_D2,
            "E": lambda: phase_FFN(x2d, yd, h3d, "wf", False),
        }
        for name in PHASES:
            p.phase = name
            ph[name]()

        p.wait("gpsimd", "dG", p.cnt["dG"])
        p.wait("sync", "dS", p.cnt["dS"])

        with nc.Block() as block:
            p.run(block)
    return nc


def kernel(L0_attn_hidden_states, L0_attn_wq, L0_attn_wk, L0_attn_wv,
           L0_attn_wo, L0_attn_gate_w, L0_attn_gate_b, L0_attn_mem_bank,
           L0_ffn_weight, L0_ffn_scale):
    nc = build_nc()
    x = np.ascontiguousarray(np.asarray(L0_attn_hidden_states, dtype=np.float32))
    common = {
        "wq": np.ascontiguousarray(np.asarray(L0_attn_wq, np.float32)),
        "wk": np.ascontiguousarray(np.asarray(L0_attn_wk, np.float32)),
        "wv": np.ascontiguousarray(np.asarray(L0_attn_wv, np.float32)),
        "wo": np.ascontiguousarray(np.asarray(L0_attn_wo, np.float32)),
        "gate_w": np.ascontiguousarray(np.asarray(L0_attn_gate_w, np.float32).reshape(1, D)),
        "gate_b": np.ascontiguousarray(np.asarray(L0_attn_gate_b, np.float32).reshape(1, 1)),
        "mem_bank": np.ascontiguousarray(np.asarray(L0_attn_mem_bank, np.float32)),
        "ffn_w": np.ascontiguousarray(np.asarray(L0_ffn_weight, np.float32)),
        "ffn_scale": np.ascontiguousarray(np.asarray(L0_ffn_scale, np.float32).reshape(1, D)),
    }
    in_maps = [dict(common, x=np.ascontiguousarray(x[b])) for b in range(8)]
    res = run_bass_kernel_spmd(nc, in_maps, core_ids=list(range(8)))
    out = np.stack([np.asarray(res.results[b]["y"]) for b in range(8)], axis=0)
    return out


if __name__ == "__main__":
    rng = np.random.default_rng(0)
    args = dict(
        L0_attn_hidden_states=rng.standard_normal((8, T, D), dtype=np.float32),
        L0_attn_wq=rng.standard_normal((D, D), dtype=np.float32) / 45,
        L0_attn_wk=rng.standard_normal((D, D), dtype=np.float32) / 45,
        L0_attn_wv=rng.standard_normal((D, D), dtype=np.float32) / 45,
        L0_attn_wo=rng.standard_normal((D, D), dtype=np.float32) / 45,
        L0_attn_gate_w=rng.standard_normal((1, D), dtype=np.float32) / 45,
        L0_attn_gate_b=np.full((1,), -2.0, np.float32),
        L0_attn_mem_bank=rng.standard_normal((M, D), dtype=np.float32),
        L0_ffn_weight=rng.standard_normal((D, D), dtype=np.float32),
        L0_ffn_scale=np.full((D,), 1 / 45.25, np.float32),
    )
    y = kernel(**args)
    print(y.shape, y.dtype, np.abs(y).mean())


# version 13
# speedup vs baseline: 1.1903x; 1.0005x over previous
import numpy as np
import ml_dtypes
import concourse.bass as bass
import concourse.mybir as mybir
from concourse.bass_utils import run_bass_kernel_spmd
from contextlib import ExitStack

F32 = mybir.dt.float32
BF16 = mybir.dt.bfloat16
AF = mybir.ActivationFunctionType
ALU = mybir.AluOpType
AXX = mybir.AxisListType.X
BF = ml_dtypes.bfloat16

D = 2048
T = 4096
M = 4096
NH = 16
NC = D // 128
NT128 = T // 128
NT512 = T // 512
NM = M // 128

RS128 = float(1.0 / np.sqrt(128.0))
RSD = float(1.0 / np.sqrt(2048.0))
ESHIFT = -30.0
EPS_FFN = 1e-7
EPS_LN = 1e-5
MAXNR = 0.99

PHASES = ["P0", "P1", "L2", "Aq", "Ak", "Av", "Az", "B", "C1", "C2", "D1", "D2", "E"]
TAPS = []
SIMPLE_FFN = False
LN_BYPASS = False
SIM_FENCE_ALL = False

ENGS = ("sync", "gpsimd", "tensor", "vector", "scalar")


DMA_OWNER = {"dS": "sync", "dG": "gpsimd"}


class Prog:
    def __init__(self, nc, sems):
        self.nc = nc
        self.sems = sems
        self.cnt = {k: 0 for k in sems}
        self.q = {e: [] for e in ENGS}
        self.lastwait = {e: {k: 0 for k in sems} for e in ENGS}
        self.fenced = {k: 0 for k in sems}

    def fence(self, sem):
        own = DMA_OWNER[sem]
        self.fenced[sem] = self.cnt[sem]
        self.wait(own, sem, self.cnt[sem])

    def wait(self, eng, sem, val):
        if val is None or val <= 0:
            return
        assert val <= self.cnt[sem], f"wait {sem}>={val}, emitted {self.cnt[sem]}"
        if sem in DMA_OWNER and eng != DMA_OWNER[sem]:
            assert val <= self.fenced[sem], \
                f"cross-engine wait {sem}>={val} not fenced (fenced={self.fenced[sem]})"
        if self.lastwait[eng][sem] >= val:
            return
        self.lastwait[eng][sem] = val
        h = self.sems[sem]
        w = lambda e, h=h, v=val: e.wait_ge(h, v)
        w.phase = self.phase
        self.q[eng].append(w)

    phase = "P?"

    def op(self, eng, fn, inc=None, n=1):
        fn.phase = self.phase
        if inc is None:
            self.q[eng].append(fn)
            return None
        self.cnt[inc] += n
        h = self.sems[inc]
        if inc in ("sA", "sV"):
            self.q[eng].append(fn)
            d = lambda e, h=h, n=n: e.drain().then_inc(h, n)
            d.phase = self.phase
            self.q[eng].append(d)
        else:
            d = lambda e, fn=fn, h=h, n=n: fn(e).then_inc(h, n)
            d.phase = self.phase
            self.q[eng].append(d)
        return self.cnt[inc]

    def dma(self, eng, out, in_, inc="dS", **kw):
        m = self.op(eng, lambda e: e.dma_start(out, in_, **kw), inc=inc, n=16)
        if SIM_FENCE_ALL:
            self.fence(inc)
        return m

    def barrier_into(self, eng):
        for s in ("sT", "sA", "sV", "sP"):
            self.wait(eng, s, self.cnt[s])

    def phase_entry(self):
        self.fence("dS")
        self.fence("dG")
        for e in ENGS:
            self.barrier_into(e)
            self.wait(e, "dS", self.fenced["dS"])
            self.wait(e, "dG", self.fenced["dG"])

    def run(self, block):
        def make(q):
            def go(e):
                i = 0
                while i < len(q):
                    ph = getattr(q[i], "phase", "P?")
                    j = i
                    with self.nc.named_scope(ph):
                        while j < len(q) and getattr(q[j], "phase", "P?") == ph:
                            q[j](e)
                            j += 1
                    i = j
            return go
        for name in ENGS:
            getattr(block, name)(make(self.q[name]))


def build_nc():
    nc = bass.Bass("TRN2", target_bir_lowering=False,
                   dynamic_dma_scratch_size=16384)

    xd = nc.dram_tensor("x", [T, D], F32, kind="ExternalInput").ap()
    wq_in = nc.dram_tensor("wq", [D, D], F32, kind="ExternalInput").ap()
    wk_in = nc.dram_tensor("wk", [D, D], F32, kind="ExternalInput").ap()
    wv_in = nc.dram_tensor("wv", [D, D], F32, kind="ExternalInput").ap()
    wo_in = nc.dram_tensor("wo", [D, D], F32, kind="ExternalInput").ap()
    gw_in = nc.dram_tensor("gate_w", [1, D], F32, kind="ExternalInput").ap()
    gb_in = nc.dram_tensor("gate_b", [1, 1], F32, kind="ExternalInput").ap()
    mem_in = nc.dram_tensor("mem_bank", [M, D], F32, kind="ExternalInput").ap()
    wf_in = nc.dram_tensor("ffn_w", [D, D], F32, kind="ExternalInput").ap()
    fs_in = nc.dram_tensor("ffn_scale", [1, D], F32, kind="ExternalInput").ap()
    yd = nc.dram_tensor("y", [T, D], F32, kind="ExternalOutput").ap()

    def scratch(name, shape, dtype):
        kind = "ExternalOutput" if name in TAPS else "Internal"
        return nc.dram_tensor(name, shape, dtype, kind=kind).ap()

    wq_bf = scratch("wq_bf", [D, D], BF16)
    wk_bf = scratch("wk_bf", [D, D], BF16)
    wv_bf = scratch("wv_bf", [D, D], BF16)
    wo_bf = scratch("wo_bf", [D, D], BF16)
    wf_bf = scratch("wf_bf", [D, D], BF16)
    wfs_bf = scratch("wfs_bf", [D, D], BF16)
    mem_bf = scratch("mem_bf", [M, D], BF16)
    x1d = scratch("x1d", [T, D], F32)
    x2d = scratch("x2d", [T, D], F32)
    h1d = scratch("h1d", [T, D], BF16)
    h2d = scratch("h2d", [T, D], BF16)
    h3d = scratch("h3d", [T, D], BF16)
    qfd = scratch("qfd", [D, T], BF16)
    kfd = scratch("kfd", [D, T], BF16)
    vtd = scratch("vtd", [T, D], BF16)
    zfd = scratch("zfd", [D, T], BF16)
    ctxd = scratch("ctxd", [D, T], BF16)
    dlocd = scratch("dlocd", [NH, T], F32)
    wspd = scratch("wspd", [T, M], BF16)
    ufd = scratch("ufd", [D, T], BF16)
    pred = scratch("pred", [D, T], BF16)
    s2dbg = scratch("s2dbg", [T, D], F32)
    smdbg = scratch("smdbg", [8 * 128, 256], F32)
    dend = scratch("dend", [T, 1], F32)
    gd = scratch("gd", [T, 1], F32)

    yd_marker = [yd]
    with ExitStack() as st:
        E = st.enter_context
        ABF = E(nc.sbuf_tensor("ABF", [128, 73728], BF16))
        AFP = E(nc.sbuf_tensor("AFP", [128, 12288], F32))
        cpool = E(nc.sbuf_tensor("cpool", [128, 8], F32))
        masks = E(nc.sbuf_tensor("masks", [128, 6 * 512], BF16))
        ones_c = E(nc.sbuf_tensor("ones_c", [128, 1], BF16))
        ones_rf = E(nc.sbuf_tensor("ones_rf", [1, 128], F32))
        gw2 = E(nc.sbuf_tensor("gw2", [128, 16], BF16))
        gcol = E(nc.sbuf_tensor("gcol", [128, 16], BF16))
        negb = E(nc.sbuf_tensor("negb", [1, 1], F32))
        gbsb = E(nc.sbuf_tensor("gbsb", [1, 1], F32))
        small = E(nc.sbuf_tensor("small", [128, 256], F32))
        PS = [E(nc.psum_tensor(f"PS{i}", [128, 512], F32)) for i in range(8)]

        CONSTS = {RS128: 0, RSD: 1, ESHIFT: 2, -1.0: 3, 1.0: 4}
        for v, i in CONSTS.items():
            nc.const_aps.register(F32, float(v), cpool[:, i:i + 1])

        sems = {}
        for name in ("dS", "dG", "sT", "sA", "sV", "sP"):
            sems[name] = E(nc.semaphore(name))
        p = Prog(nc, sems)

        (b_s1, b_q, b_mean, b_var, b_var2, b_rstd, b_negmr) = (0, 8, 16, 24, 32, 40, 48)
        b_nsq = 64
        (b_S, b_v, b_tk, b_f1, b_f2, b_r1, b_fac, b_on, b_orc, b_clip, b_cc) = (
            96, 104, 112, 120, 128, 136, 144, 152, 160, 168, 176)
        c_t8m, c_negm, c_dc = 184, 192, 196

        def sc(c, k=1):
            return small[:, c:c + k]

        def bc(base, P, k=4):
            return small[:, base + P * 4: base + P * 4 + k]

        ready = {}

        def emit_cast(key, dst, srct):
            ready[key] = p.dma("gpsimd", dst[:, :], srct[:, :], "dG")
            p.fence("dG")
        last_mH = [0]
        last_mV = [0]

        lnguard = {}

        def emit_ln_chain(P, guard_key):
            p.op("vector", lambda e: e.tensor_scalar(
                bc(b_mean, P), bc(b_s1, P), 1.0 / D, None, ALU.mult), inc="sV")
            p.op("vector", lambda e: e.tensor_scalar(
                bc(b_q, P), bc(b_q, P), 1.0 / D, None, ALU.mult), inc="sV")
            p.op("vector", lambda e: e.tensor_tensor(
                bc(b_var, P), bc(b_mean, P), bc(b_mean, P), ALU.mult), inc="sV")
            mv2 = p.op("vector", lambda e: e.scalar_tensor_tensor(
                bc(b_var2, P), bc(b_q, P), EPS_LN, bc(b_var, P),
                op0=ALU.add, op1=ALU.subtract), inc="sV")
            p.wait("vector", "sA", lnguard.get(("hnorm", P), 0))
            p.wait("scalar", "sV", mv2)
            mL1 = p.op("scalar", lambda e: e.activation(
                bc(b_rstd, P), bc(b_var2, P), AF.Ln), inc="sA")
            p.wait("vector", "sA", mL1)
            mNg = p.op("vector", lambda e: e.tensor_scalar(
                bc(b_rstd, P), bc(b_rstd, P), -0.5, None, ALU.mult), inc="sV")
            p.wait("scalar", "sV", mNg)
            mR = p.op("scalar", lambda e: e.activation(
                bc(b_rstd, P), bc(b_rstd, P), AF.Exp), inc="sA")
            p.wait("vector", "sA", mR)
            mNM = p.op("vector", lambda e: e.scalar_tensor_tensor(
                bc(b_negmr, P), bc(b_mean, P), -1.0, bc(b_rstd, P),
                op0=ALU.mult, op1=ALU.mult), inc="sV")
            lnguard[("var2", P)] = mv2
            return mNM

        def emit_factor_chain(P):
            nview = small[:, b_nsq + P * 16: b_nsq + P * 16 + 16].rearrange(
                "p (s k) -> p s k", k=4)
            mNr = p.op("vector", lambda e: e.reduce_sum(
                bc(b_S, P), nview, AXX), inc="sV")
            p.wait("scalar", "sV", mNr)
            mLn = p.op("scalar", lambda e: e.activation(
                bc(b_v, P), bc(b_S, P), AF.Ln), inc="sA")
            p.wait("vector", "sA", mLn)
            mHf = p.op("vector", lambda e: e.tensor_scalar(
                bc(b_v, P), bc(b_v, P), 0.5, None, ALU.mult), inc="sV")
            p.wait("scalar", "sV", mHf)
            mSq = p.op("scalar", lambda e: e.activation(
                bc(b_v, P), bc(b_v, P), AF.Exp), inc="sA")
            p.wait("vector", "sA", mSq)
            p.op("vector", lambda e: e.tensor_scalar(
                bc(b_v, P), bc(b_v, P), EPS_FFN, None, ALU.max), inc="sV")
            mT = p.op("vector", lambda e: e.tensor_scalar(
                bc(b_tk, P), bc(b_v, P), -2.0, None, ALU.mult), inc="sV")
            p.wait("scalar", "sV", mT)
            mEx = p.op("scalar", lambda e: e.activation(
                bc(b_tk, P), bc(b_tk, P), AF.Exp), inc="sA")
            p.wait("vector", "sA", mEx)
            p.op("vector", lambda e: e.tensor_scalar(
                bc(b_f1, P), bc(b_tk, P), -1.0, 1.0, ALU.mult, ALU.add), inc="sV")
            p.op("vector", lambda e: e.tensor_scalar(
                bc(b_tk, P), bc(b_tk, P), 1.0, None, ALU.add), inc="sV")
            p.op("vector", lambda e: e.reciprocal(bc(b_tk, P), bc(b_tk, P)), inc="sV")
            p.op("vector", lambda e: e.tensor_tensor(
                bc(b_f2, P), bc(b_f1, P), bc(b_tk, P), ALU.mult), inc="sV")
            p.op("vector", lambda e: e.tensor_scalar(
                bc(b_r1, P), bc(b_v, P), EPS_FFN, None, ALU.add), inc="sV")
            p.op("vector", lambda e: e.reciprocal(bc(b_r1, P), bc(b_r1, P)), inc="sV")
            p.op("vector", lambda e: e.tensor_tensor(
                bc(b_fac, P), bc(b_f2, P), bc(b_r1, P), ALU.mult), inc="sV")
            p.op("vector", lambda e: e.tensor_tensor(
                bc(b_on, P), bc(b_fac, P), bc(b_v, P), ALU.mult), inc="sV")
            p.op("vector", lambda e: e.tensor_scalar(
                bc(b_on, P), bc(b_on, P), EPS_FFN, None, ALU.max), inc="sV")
            p.op("vector", lambda e: e.reciprocal(bc(b_orc, P), bc(b_on, P)), inc="sV")
            p.op("vector", lambda e: e.tensor_scalar(
                bc(b_clip, P), bc(b_orc, P), MAXNR, 1.0, ALU.mult, ALU.min), inc="sV")
            p.op("vector", lambda e: e.tensor_tensor(
                bc(b_cc, P), bc(b_fac, P), bc(b_clip, P), ALU.mult), inc="sV")
            mC = p.op("vector", lambda e: e.tensor_scalar(
                bc(b_cc, P), bc(b_cc, P), 0.5, None, ALU.mult), inc="sV")
            return mC

        def phase_P0():
            for v, i in CONSTS.items():
                p.op("gpsimd", lambda e, i=i, v=v: e.memset(cpool[:, i:i + 1], float(v)))
            p.op("gpsimd", lambda e: e.memset(ones_c[:, :], 1.0))
            p.op("gpsimd", lambda e: e.memset(ones_rf[:, :], 1.0))
            mt = ABF[:, 0:512]
            mms = p.op("gpsimd", lambda e: e.memset(mt[:, :], 1.0), inc="sP")
            p.wait("gpsimd", "sP", mms)
            for c in range(6):
                off = (c - 2) * 128
                m1 = p.op("gpsimd", lambda e, c=c, off=off: e.affine_select(
                    masks[:, c * 512:(c + 1) * 512], mt[:, :], [[1, 512]],
                    compare_op=ALU.is_ge, fill=0.0, base=-off, channel_multiplier=-1),
                    inc="sP")
                p.wait("gpsimd", "sP", m1)
                m2 = p.op("gpsimd", lambda e, c=c, off=off: e.affine_select(
                    masks[:, c * 512:(c + 1) * 512], masks[:, c * 512:(c + 1) * 512],
                    [[-1, 512]], compare_op=ALU.is_ge, fill=0.0,
                    base=255 + off, channel_multiplier=1), inc="sP")
                p.wait("gpsimd", "sP", m2)
            mP0 = p.op("gpsimd", lambda e: e.engine_nop(), inc="sP")
            for eng in ("scalar", "vector", "tensor"):
                p.wait(eng, "sP", mP0)

            gw_f = AFP[0:1, 0:D]
            fs_row = AFP[0:1, D:2 * D]
            mGW = p.dma("gpsimd", gw_f, gw_in[:, :], "dG")
            p.fence("dG")
            mGB = p.dma("gpsimd", gbsb[0:1, 0:1], gb_in[:, :], "dG")
            p.fence("dG")
            mFS = p.dma("gpsimd", fs_row, fs_in[:, :], "dG")
            p.fence("dG")
            ready["wf"] = p.dma("gpsimd", wf_bf[:, :], wf_in[:, :], "dG")
            p.fence("dG")
            for dst, src, key in ((wo_bf, wo_in, "wo"),):
                ready[key] = p.dma("gpsimd", dst[:, :], src[:, :], "dG")
                p.fence("dG")

            p.wait("vector", "dG", mGB)
            p.op("vector", lambda e: e.tensor_scalar(
                negb[0:1, 0:1], gbsb[0:1, 0:1], -1.0, None, ALU.mult), inc="sV")

            mSCLB = p.cnt["sV"]

            id11 = cpool[0:1, CONSTS[1.0]:CONSTS[1.0] + 1]
            p.wait("tensor", "dG", mGW)
            p.wait("tensor", "sV", mSCLB)
            gev = {}
            for i in range(NC):
                bank = 4 + (i % 2)
                if i >= 2:
                    p.wait("tensor", "sV", gev[i - 2])
                mm = p.op("tensor", lambda e, i=i, bank=bank: e.transpose(
                    PS[bank][:, 0:1], gw_f[0:1, i * 128:(i + 1) * 128], id11), inc="sT")
                p.wait("vector", "sT", mm)
                gev[i] = p.op("vector", lambda e, i=i, bank=bank: e.tensor_copy(
                    gcol[:, i:i + 1], PS[bank][:, 0:1]), inc="sV")
            mGCOL = p.cnt["sV"]

            wob = [ABF[:, 2048 + i * 2048: 2048 + (i + 1) * 2048] for i in range(2)]
            p.fence("dG")
            p.wait("sync", "dG", ready["wo"])
            p.wait("tensor", "sV", mGCOL)
            lastmm = {}
            for i in range(NC):
                buf = wob[i % 2]
                if i >= 2:
                    p.wait("sync", "sT", lastmm[i - 2])
                mL = p.dma("sync", buf, wo_bf[i * 128:(i + 1) * 128, :], "dS")
                p.fence("dS")
                p.wait("tensor", "dS", mL)
                for jj in range(16):
                    j, col = jj % 8, jj // 8
                    mm = p.op("tensor", lambda e, buf=buf, jj=jj, j=j, col=col, i=i: e.matmul(
                        PS[j][:, col:col + 1], buf[:, jj * 128:(jj + 1) * 128],
                        gcol[:, i:i + 1],
                        start=(i == 0), stop=(i == NC - 1)), inc="sT")
                lastmm[i] = mm
            for jj in range(16):
                j, col = jj % 8, jj // 8
                p.wait("vector", "sT", lastmm[NC - 1])
                p.op("vector", lambda e, j=j, jj=jj, col=col: e.tensor_copy(
                    gw2[:, jj:jj + 1], PS[j][:, col:col + 1]), inc="sV")
            ready["gw2"] = p.cnt["sV"]

            fscol = gcol
            fscol = ABF[:, 0:16].bitcast(F32) if False else None
            p.wait("tensor", "dG", mFS)
            p.wait("tensor", "sV", ready["gw2"])
            fev = {}
            for i in range(NC):
                bank = 4 + (i % 2)
                if i >= 2:
                    p.wait("tensor", "sV", fev[i - 2])
                mm = p.op("tensor", lambda e, i=i, bank=bank: e.transpose(
                    PS[bank][:, 0:1], fs_row[0:1, i * 128:(i + 1) * 128], id11), inc="sT")
                p.wait("vector", "sT", mm)
                fev[i] = p.op("vector", lambda e, i=i, bank=bank: e.tensor_copy(
                    small[:, 200 + i: 201 + i], PS[bank][:, 0:1]), inc="sV")
            ready["afp_free"] = p.cnt["sT"]
            p.wait("sync", "dG", ready["wf"])
            p.wait("sync", "sT", p.cnt["sT"])
            wsb2 = [ABF[:, 2048 + i * 2048: 2048 + (i + 1) * 2048] for i in range(2)]
            wst_done = {}
            for i in range(NC):
                buf = wsb2[i % 2]
                if i >= 2:
                    p.wait("sync", "dG", wst_done[i - 2])
                mL = p.dma("sync", buf, wf_bf[:, i * 128:(i + 1) * 128], "dS",
                           transpose=True)
                p.fence("dS")
                p.wait("vector", "dS", mL)
                p.wait("vector", "sV", fev[NC - 1])
                mS = p.op("vector", lambda e, buf=buf, i=i: e.tensor_scalar(
                    buf, buf, small[:, 200 + i: 201 + i], None, ALU.mult), inc="sV")
                p.wait("gpsimd", "sV", mS)
                wst_done[i] = p.dma("gpsimd", wfs_bf[i * 128:(i + 1) * 128, :], buf, "dG")
                p.fence("dG")
            ready["wfs"] = p.cnt["dG"]

        def phase_FFN(src, dst, hdram, wkey, emit_h, entry=True):
            Wf = ABF[:, 0:32768]
            htm = ABF[:, 32768:34816]
            hfm = [ABF[:, 34816 + i * 8192: 34816 + (i + 1) * 8192] for i in range(2)]
            s2b = [ABF[:, 51200 + i * 8192: 51200 + (i + 1) * 8192] for i in range(2)]
            sqb = ABF[:, 67584:69632]
            ob = ABF[:, 69632:73728].bitcast(F32)
            xln = [AFP[:, i * 2048:(i + 1) * 2048] for i in range(4)]
            xr = [AFP[:, 8192 + i * 2048: 8192 + (i + 1) * 2048] for i in range(2)]

            if entry:
                p.phase_entry()
            lnguard.clear()

            hist = {}
            mXd = {}

            def load_xln(g):
                if g >= NT512 * 4:
                    return
                if not entry and g < 4:
                    p.wait("sync", "sT", ready.get("afp_free", 0))
                if g >= 4:
                    p.wait("sync", "sA", hist[("hn", g - 4)])
                mXd[g] = p.dma("sync", xln[g % 4],
                               src[g * 128:(g + 1) * 128, :], "dS")
                p.fence("dS")

            def load_xr(g):
                if g >= NT512 * 4:
                    return
                if g >= 2:
                    p.wait("gpsimd", "sV", hist[("resid", g - 2)])
                mXd[("r", g)] = p.dma("gpsimd", xr[g % 2],
                                      src[g * 128:(g + 1) * 128, :], "dG")
                p.fence("dG")

            def emF1(t):
                P = t % 2
                if t >= 2:
                    p.wait("scalar", "sV", lnguard.get(("var2", P), 0))
                for sub in range(4):
                    g = t * 4 + sub
                    p.wait("vector", "dS", mXd[g])
                    p.wait("scalar", "dS", mXd[g])
                    p.op("vector", lambda e, g=g, sub=sub, P=P: e.reduce_sum(
                        small[:, b_s1 + P * 4 + sub: b_s1 + P * 4 + sub + 1],
                        xln[g % 4], AXX), inc="sV")
                    if sub >= 1:
                        p.wait("scalar", "sV", hist[(t, "qred", sub - 1)])
                    elif t >= 1:
                        p.wait("scalar", "sV", hist[(t - 1, "qred", 3)])
                        p.wait("scalar", "dG", hist[(t - 1, "hst", 3)])
                    mSq = p.op("scalar", lambda e, g=g: e.activation(
                        htm, xln[g % 4], AF.Square), inc="sA")
                    p.wait("vector", "sA", mSq)
                    hist[(t, "qred", sub)] = p.op("vector", lambda e, sub=sub, P=P: e.reduce_sum(
                        small[:, b_q + P * 4 + sub: b_q + P * 4 + sub + 1],
                        htm, AXX), inc="sV")
                mNM = emit_ln_chain(P, None)
                p.wait("scalar", "sV", mNM)
                mHl = None
                for sub in range(4):
                    g = t * 4 + sub
                    if sub >= 1:
                        p.wait("scalar", "dG", hist[(t, "hst", sub - 1)])
                    elif t >= 1:
                        p.wait("scalar", "dG", hist[(t - 1, "hst", 3)])
                    mHl = p.op("scalar", lambda e, g=g, sub=sub, P=P: e.activation(
                        htm, xln[g % 4], AF.Identity,
                        bias=small[:, b_negmr + P * 4 + sub: b_negmr + P * 4 + sub + 1],
                        scale=small[:, b_rstd + P * 4 + sub: b_rstd + P * 4 + sub + 1]),
                        inc="sA")
                    hist[("hn", g)] = mHl
                    p.wait("gpsimd", "sA", mHl)
                    hist[(t, "hst", sub)] = p.dma(
                        "gpsimd", hdram[g * 128:(g + 1) * 128, :], htm, "dG")
                    p.fence("dG")
                lnguard[("hnorm", P)] = mHl
                for sub in range(4):
                    load_xln((t + 1) * 4 + sub)

            def emF2(t):
                P = t % 2
                hf = hfm[P]
                p.wait("sync", "dG", hist[(t, "hst", 3)])
                if t >= 2:
                    p.wait("sync", "sT", hist[(t - 2, "mml")])
                mHrd = 0
                for i in range(NC):
                    mHrd = p.dma("sync", hf[:, i * 512:(i + 1) * 512],
                                 hdram[t * 512:(t + 1) * 512, i * 128:(i + 1) * 128],
                                 "dS", transpose=True)
                p.fence("dS")
                if t == 0:
                    load_xr(0)
                    load_xr(1)
                p.wait("tensor", "dS", mW)
                p.wait("tensor", "dS", mHrd)
                s2 = s2b[P]
                if t >= 2:
                    p.wait("scalar", "sV", hist[("resid", (t - 2) * 4 + 3)])
                mm = None
                for gi in range(16):
                    sub, kb = gi // 4, gi % 4
                    bank = gi % 4
                    if t * 16 + gi >= 4:
                        p.wait("tensor", "sA", hist.get((t, "ev", gi - 4),
                                                        hist.get((t - 1, "ev", gi + 12), 0)))
                    for n in range(NC):
                        mm = p.op("tensor", lambda e, sub=sub, kb=kb, n=n, hf=hf, bank=bank: e.matmul(
                            PS[bank][:, :],
                            hf[:, n * 512 + sub * 128: n * 512 + (sub + 1) * 128],
                            Wf[:, n * 2048 + kb * 512: n * 2048 + (kb + 1) * 512],
                            start=(n == 0), stop=(n == NC - 1)), inc="sT")
                    hist[(t, "mm", gi)] = mm
                    p.wait("scalar", "sT", mm)
                    mE = p.op("scalar", lambda e, gi=gi, s2=s2, bank=bank: e.activation(
                        s2[:, gi * 512:(gi + 1) * 512], PS[bank][:, :], AF.Copy), inc="sA")
                    hist[(t, "ev", gi)] = mE
                    if gi >= 1:
                        p.wait("scalar", "sV", hist[(t, "nqr", gi - 1)])
                    elif t >= 1:
                        p.wait("scalar", "sV", hist[(t - 1, "nqr", 15)])
                    mSq2 = p.op("scalar", lambda e, gi=gi, kb=kb, s2=s2: e.activation(
                        sqb[:, kb * 512:(kb + 1) * 512],
                        s2[:, gi * 512:(gi + 1) * 512], AF.Square), inc="sA")
                    p.wait("vector", "sA", mSq2)
                    hist[(t, "nqr", gi)] = p.op("vector", lambda e, gi=gi, kb=kb, P=P: e.reduce_sum(
                        small[:, b_nsq + P * 16 + gi: b_nsq + P * 16 + gi + 1],
                        sqb[:, kb * 512:(kb + 1) * 512], AXX), inc="sV")
                hist[(t, "mml")] = mm
                mC = emit_factor_chain(P)
                for sub in range(4):
                    g = t * 4 + sub
                    if g >= 1:
                        prev = hist[(t, "ost", sub - 1)] if sub >= 1 else hist[(t - 1, "ost", 3)]
                        p.wait("vector", "dG", prev)
                    p.wait("vector", "dG", mXd[("r", g)])
                    mRe = None
                    for kb in range(4):
                        gi = sub * 4 + kb
                        mRe = p.op("vector", lambda e, g=g, gi=gi, kb=kb, sub=sub, s2=s2, P=P:
                                   e.scalar_tensor_tensor(
                                       ob[:, kb * 512:(kb + 1) * 512],
                                       s2[:, gi * 512:(gi + 1) * 512],
                                       small[:, b_cc + P * 4 + sub: b_cc + P * 4 + sub + 1],
                                       xr[g % 2][:, kb * 512:(kb + 1) * 512],
                                       op0=ALU.mult, op1=ALU.add), inc="sV")
                    hist[("resid", g)] = mRe
                    p.wait("gpsimd", "sV", mRe)
                    hist[(t, "ost", sub)] = p.dma(
                        "gpsimd", dst[g * 128:(g + 1) * 128, :], ob, "dG")
                    p.fence("dG")
                    load_xr(g + 2)

            if not entry:
                for gg in range(4):
                    load_xln(gg)
            p.wait("sync", "dG", ready["wfs"])
            mW = 0
            for i in range(NC):
                mW = p.dma("sync", Wf[:, i * 2048:(i + 1) * 2048],
                           wfs_bf[:, i * 128:(i + 1) * 128], "dS", transpose=True)
            p.fence("dS")
            if entry:
                for gg in range(4):
                    load_xln(gg)
            for t in range(NT512 + 2):
                if t < NT512:
                    emF1(t)
                if t >= 2:
                    emF2(t - 2)
            ready["p1_mm"] = p.cnt["sT"]
            return p.cnt["dG"]

        def preload_projW(key, wsrc):
            Wt = ABF[:, 0:32768]
            p.wait("sync", "sT", ready["p1_mm"])
            p.wait("sync", "dG", ready[key])
            mW = 0
            for i in range(NC):
                mW = p.dma("sync", Wt[:, i * 2048:(i + 1) * 2048],
                           wsrc[:, i * 128:(i + 1) * 128], "dS", transpose=True)
            p.fence("dS")
            ready[("mW", key)] = mW

        def phase_L2(src, hdst):
            htm = [ABF[:, 55296:57344], ABF[:, 57344:59392]]
            sqb2 = [ABF[:, 67584:69632], ABF[:, 69632:71680]]
            xln = [AFP[:, i * 2048:(i + 1) * 2048] for i in range(4)]

            p.phase_entry()
            lnguard.clear()

            hist = {}
            for t in range(NT512):
                P = t % 2
                mXs = []
                for sub in range(4):
                    ts_ = t * 4 + sub
                    if t >= 1:
                        p.wait("sync", "sA", hist[(t - 1, "hn", sub)])
                        p.wait("sync", "sV", hist[(t - 1, "red", sub)])
                    mXs.append(p.dma("sync", xln[sub],
                                     src[ts_ * 128:(ts_ + 1) * 128, :], "dS"))
                p.fence("dS")
                if t >= 2:
                    p.wait("scalar", "sV", lnguard.get(("var2", P), 0))
                for sub in range(4):
                    p.wait("vector", "dS", mXs[sub])
                    p.wait("scalar", "dS", mXs[sub])
                    p.op("vector", lambda e, sub=sub, P=P: e.reduce_sum(
                        small[:, b_s1 + P * 4 + sub: b_s1 + P * 4 + sub + 1],
                        xln[sub], AXX), inc="sV")
                    sq = sqb2[sub % 2]
                    if sub >= 2:
                        p.wait("scalar", "sV", hist[(t, "qred", sub - 2)])
                    elif t >= 1:
                        p.wait("scalar", "sV", hist[(t - 1, "qred", sub + 2)])
                    mSq = p.op("scalar", lambda e, sub=sub, sq=sq: e.activation(
                        sq, xln[sub], AF.Square), inc="sA")
                    p.wait("vector", "sA", mSq)
                    mqq = p.op("vector", lambda e, sub=sub, sq=sq, P=P: e.reduce_sum(
                        small[:, b_q + P * 4 + sub: b_q + P * 4 + sub + 1],
                        sq, AXX), inc="sV")
                    hist[(t, "qred", sub)] = mqq
                    hist[(t, "red", sub)] = mqq
                p.wait("vector", "sA", p.cnt["sA"])
                mNM = emit_ln_chain(P, None)
                p.wait("scalar", "sV", mNM)
                for sub in range(4):
                    ts_ = t * 4 + sub
                    ht = htm[sub % 2]
                    if sub >= 2:
                        p.wait("scalar", "dG", hist[(t, "hst", sub - 2)])
                    elif t >= 1:
                        p.wait("scalar", "dG", hist[(t - 1, "hst", sub + 2)])
                    mH = p.op("scalar", lambda e, sub=sub, ht=ht, P=P: e.activation(
                        ht, xln[sub], AF.Identity,
                        bias=small[:, b_negmr + P * 4 + sub: b_negmr + P * 4 + sub + 1],
                        scale=small[:, b_rstd + P * 4 + sub: b_rstd + P * 4 + sub + 1]),
                        inc="sA")
                    hist[(t, "hn", sub)] = mH
                    p.wait("gpsimd", "sA", mH)
                    hist[(t, "hst", sub)] = p.dma(
                        "gpsimd", hdst[ts_ * 128:(ts_ + 1) * 128, :], ht, "dG")
                    p.fence("dG")
                lnguard[("hnorm", P)] = hist[(t, "hn", 3)]
                ready[("h2t", t)] = hist[(t, "hst", 3)]
            ready["h2"] = p.cnt["dG"]

        def phase_proj(kind, wsrc, outd, entry=True):
            Wt = ABF[:, 0:32768]
            hfm = [ABF[:, 32768 + i * 8192: 32768 + (i + 1) * 8192] for i in range(2)]
            stg = [ABF[:, 51200 + i * 512: 51200 + (i + 1) * 512] for i in range(4)]

            if entry:
                p.phase_entry()
                mW = 0
                if kind in ("q", "k", "v"):
                    p.wait("sync", "dG", ready[{"q": "wq", "k": "wk", "v": "wv"}[kind]])
                    for i in range(NC):
                        mW = p.dma("sync", Wt[:, i * 2048:(i + 1) * 2048],
                                   wsrc[:, i * 128:(i + 1) * 128], "dS", transpose=True)
                else:
                    p.wait("sync", "dG", ready["wk"])
                    for i in range(NC):
                        mW = p.dma("sync", Wt[:, i * 2048:(i + 1) * 2048],
                                   wsrc[i * 128:(i + 1) * 128, :], "dS")
                p.fence("dS")
                p.wait("sync", "dG", ready["h2"])
            else:
                mW = ready[("mW", {"q": "wq"}[kind])]

            hist = {}
            g = 0
            for t in range(NT512):
                hf = hfm[t % 2]
                if not entry:
                    p.wait("sync", "dG", ready[("h2t", t)])
                if t >= 2:
                    p.wait("sync", "sT", hist[("mml", t - 2)])
                mHrd = 0
                for i in range(NC):
                    mHrd = p.dma("sync", hf[:, i * 512:(i + 1) * 512],
                                 h2d[t * 512:(t + 1) * 512, i * 128:(i + 1) * 128],
                                 "dS", transpose=True)
                p.fence("dS")
                p.wait("tensor", "dS", mW)
                p.wait("tensor", "dS", mHrd)
                if kind in ("q", "k", "z"):
                    for m in range(NC):
                        bank = g % 4
                        if g >= 4:
                            p.wait("tensor", "sA", hist[("ev", g - 4)])
                        mm = None
                        for n in range(NC):
                            mm = p.op("tensor", lambda e, m=m, n=n, hf=hf, bank=bank: e.matmul(
                                PS[bank][:, :],
                                Wt[:, n * 2048 + m * 128: n * 2048 + (m + 1) * 128],
                                hf[:, n * 512:(n + 1) * 512],
                                start=(n == 0), stop=(n == NC - 1)), inc="sT")
                        hist[("mm", g)] = mm
                        p.wait("scalar", "sT", mm)
                        if g >= 4:
                            p.wait("scalar", "dG", hist[("st", g - 4)])
                        sg = stg[g % 4]
                        mE = p.op("scalar", lambda e, sg=sg, bank=bank: e.activation(
                            sg, PS[bank][:, :], AF.Copy), inc="sA")
                        hist[("ev", g)] = mE
                        p.wait("gpsimd", "sA", mE)
                        hist[("st", g)] = p.dma(
                            "gpsimd", outd[m * 128:(m + 1) * 128, t * 512:(t + 1) * 512],
                            sg, "dG")
                        p.fence("dG")
                        g += 1
                else:
                    for sidx in range(4):
                        for obk in range(4):
                            bank = g % 4
                            if g >= 4:
                                p.wait("tensor", "sA", hist[("ev", g - 4)])
                            mm = None
                            for n in range(NC):
                                mm = p.op("tensor", lambda e, sidx=sidx, obk=obk, n=n, hf=hf, bank=bank: e.matmul(
                                    PS[bank][:, :],
                                    hf[:, n * 512 + sidx * 128: n * 512 + (sidx + 1) * 128],
                                    Wt[:, n * 2048 + obk * 512: n * 2048 + (obk + 1) * 512],
                                    start=(n == 0), stop=(n == NC - 1)), inc="sT")
                            hist[("mm", g)] = mm
                            p.wait("scalar", "sT", mm)
                            if g >= 4:
                                p.wait("scalar", "dG", hist[("st", g - 4)])
                            sg = stg[g % 4]
                            mE = p.op("scalar", lambda e, sg=sg, bank=bank: e.activation(
                                sg, PS[bank][:, :], AF.Copy), inc="sA")
                            hist[("ev", g)] = mE
                            p.wait("gpsimd", "sA", mE)
                            hist[("st", g)] = p.dma(
                                "gpsimd",
                                outd[t * 512 + sidx * 128: t * 512 + (sidx + 1) * 128,
                                     obk * 512:(obk + 1) * 512], sg, "dG")
                            p.fence("dG")
                            g += 1
                hist[("mml", t)] = hist[("mm", g - 1)]
            ready[kind + "done"] = p.cnt["dG"]

        def phase_B():
            qbs = [ABF[:, i * 8192:(i + 1) * 8192] for i in range(2)]
            kbs = [ABF[:, 16384 + i * 12288: 16384 + (i + 1) * 12288] for i in range(2)]
            vb = ABF[:, 40960:53248]
            eb = [ABF[:, 53248 + i * 3072: 53248 + (i + 1) * 3072] for i in range(2)]
            wb = [ABF[:, 59392 + i * 3072: 59392 + (i + 1) * 3072] for i in range(2)]
            ctxs = [ABF[:, 65536 + i * 512: 65536 + (i + 1) * 512] for i in range(2)]
            dr_t = [AFP[0:1, 512 + i * 512: 512 + (i + 1) * 512] for i in range(2)]

            p.phase_entry()
            p.wait("sync", "dG", ready["qdone"])
            p.wait("sync", "dG", ready["kdone"])
            p.wait("sync", "dG", ready["vdone"])

            hist = {}
            scg = 0
            for qt in range(NT512):
                base_kb = 4 * qt - 2
                chunks = [c for c in range(6) if 0 <= base_kb + c < NT128]
                qp = qt % 2
                qb, kb = qbs[qp], kbs[qp]
                if qt >= 2:
                    p.wait("sync", "sT", hist[("scl", qt - 2)])
                mL = 0
                for h in range(NH):
                    mL = p.dma("sync", qb[:, h * 512:(h + 1) * 512],
                               qfd[h * 128:(h + 1) * 128, qt * 512:(qt + 1) * 512], "dS")
                for h in range(NH):
                    for c in chunks:
                        ki = base_kb + c
                        mL = p.dma("sync", kb[:, h * 768 + c * 128: h * 768 + (c + 1) * 128],
                                   kfd[h * 128:(h + 1) * 128, ki * 128:(ki + 1) * 128], "dS")
                mQK = mL
                if qt >= 1:
                    p.wait("sync", "sT", hist[("pvl", qt - 1, NH - 1)])
                for c in chunks:
                    ki = base_kb + c
                    mL = p.dma("sync", vb[:, c * 2048:(c + 1) * 2048],
                               vtd[ki * 128:(ki + 1) * 128, :], "dS")
                p.fence("dS")
                p.wait("tensor", "dS", mQK)

                def emA(h):
                    nonlocal scg
                    hp = h % 2
                    key_prev = (qt, h - 2) if h >= 2 else ((qt - 1, h + 14) if qt >= 1 else None)
                    for ci, c in enumerate(chunks):
                        bank = scg % 2
                        if scg >= 2:
                            p.wait("tensor", "sA", hist[("exp", scg - 2)])
                        mm = p.op("tensor", lambda e, h=h, c=c, bank=bank, kb=kb, qb=qb: e.matmul(
                            PS[bank][:, :],
                            kb[:, h * 768 + c * 128: h * 768 + (c + 1) * 128],
                            qb[:, h * 512:(h + 1) * 512], start=True, stop=True), inc="sT")
                        p.wait("scalar", "sT", mm)
                        if key_prev is not None:
                            p.wait("scalar", "sV", hist.get(("wm", *key_prev, c), 0))
                        mex = p.op("scalar", lambda e, c=c, hp=hp, bank=bank: e.activation(
                            eb[hp][:, c * 512:(c + 1) * 512], PS[bank][:, :], AF.Exp,
                            bias=ESHIFT, scale=RS128), inc="sA")
                        hist[("exp", scg)] = mex
                        p.wait("vector", "sA", mex)
                        if key_prev is not None:
                            p.wait("vector", "sT", hist.get(("pvl", *key_prev), 0))
                        mw = p.op("vector", lambda e, c=c, hp=hp: e.tensor_tensor(
                            wb[hp][:, c * 512:(c + 1) * 512],
                            eb[hp][:, c * 512:(c + 1) * 512],
                            masks[:, c * 512:(c + 1) * 512], ALU.mult), inc="sV")
                        hist[("wm", qt, h, c)] = mw
                        hist[("scl", qt)] = mm
                        scg += 1

                def emB(h):
                    hp = h % 2
                    key_prev = (qt, h - 2) if h >= 2 else ((qt - 1, h + 14) if qt >= 1 else None)
                    if h == 0:
                        p.wait("tensor", "dS", mL)
                    if key_prev is not None:
                        p.wait("tensor", "sV", hist.get(("drv", *key_prev), 0))
                        p.wait("tensor", "sA", hist.get(("cev", *key_prev), 0))
                    mde = mpv = None
                    for ci, c in enumerate(chunks):
                        p.wait("tensor", "sV", hist[("wm", qt, h, c)])
                        first, last = ci == 0, ci == len(chunks) - 1
                        mde = p.op("tensor", lambda e, c=c, hp=hp, first=first, last=last: e.matmul(
                            PS[2 + hp][0:1, :], ones_c[:, :],
                            wb[hp][:, c * 512:(c + 1) * 512],
                            start=first, stop=last), inc="sT")
                        mpv = p.op("tensor", lambda e, c=c, h=h, hp=hp, first=first, last=last: e.matmul(
                            PS[5 + hp][:, :],
                            vb[:, c * 2048 + h * 128: c * 2048 + (h + 1) * 128],
                            wb[hp][:, c * 512:(c + 1) * 512],
                            start=first, stop=last), inc="sT")
                    hist[("pvl", qt, h)] = mpv
                    hist[("del", qt, h)] = mde

                def emC(h):
                    hp = h % 2
                    key_prev = (qt, h - 2) if h >= 2 else ((qt - 1, h + 14) if qt >= 1 else None)
                    p.wait("vector", "sT", hist[("del", qt, h)])
                    if key_prev is not None:
                        p.wait("vector", "dG", hist.get(("dst", *key_prev), 0))
                    mdr = p.op("vector", lambda e, hp=hp: e.tensor_copy(
                        dr_t[hp], PS[2 + hp][0:1, :]), inc="sV")
                    hist[("drv", qt, h)] = mdr
                    ctx = ctxs[hp]
                    p.wait("scalar", "sT", hist[("pvl", qt, h)])
                    if key_prev is not None:
                        p.wait("scalar", "dG", hist.get(("cst", *key_prev), 0))
                    mcv = p.op("scalar", lambda e, ctx=ctx, hp=hp: e.activation(
                        ctx, PS[5 + hp][:, :], AF.Copy), inc="sA")
                    hist[("cev", qt, h)] = mcv
                    p.wait("gpsimd", "sA", mcv)
                    hist[("cst", qt, h)] = p.dma(
                        "gpsimd", ctxd[h * 128:(h + 1) * 128, qt * 512:(qt + 1) * 512],
                        ctx, "dG")
                    p.wait("gpsimd", "sV", mdr)
                    hist[("dst", qt, h)] = p.dma(
                        "gpsimd", dlocd[h:h + 1, qt * 512:(qt + 1) * 512],
                        dr_t[hp], "dG")
                    p.fence("dG")

                for k in range(NH + 2):
                    if k < NH:
                        emA(k)
                    if 1 <= k <= NH:
                        emB(k - 1)
                    if k >= 2:
                        emC(k - 2)
            ready["ctx"] = p.cnt["dG"]
            ready["dloc"] = p.cnt["dG"]

        def phase_C1():
            memT = ABF[:, 0:65536]
            zb = [ABF[:, 65536 + i * 2048: 65536 + (i + 1) * 2048] for i in range(2)]
            ew = ABF[:, 69632:73728]
            wsbs = [AFP[:, 8192:9216].bitcast(BF16), AFP[:, 9216:10240].bitcast(BF16)]
            sims = [AFP[:, i * 4096:(i + 1) * 4096] for i in range(2)]

            p.phase_entry()
            p.wait("sync", "dG", ready["mem"])
            mMT = 0
            for i in range(NC):
                mMT = p.dma("sync", memT[:, i * 4096:(i + 1) * 4096],
                            mem_bf[:, i * 128:(i + 1) * 128], "dS", transpose=True)
            p.fence("dS")
            p.wait("sync", "dG", ready["zdone"])

            hist = {}
            for t in range(NT128):
                zt = zb[t % 2]
                if t >= 2:
                    p.wait("sync", "sT", hist[("mml", t - 2)])
                mZ = 0
                for i in range(NC):
                    mZ = p.dma("sync", zt[:, i * 128:(i + 1) * 128],
                               zfd[i * 128:(i + 1) * 128, t * 128:(t + 1) * 128], "dS")
                p.fence("dS")
                p.wait("tensor", "dS", mMT)
                p.wait("tensor", "dS", mZ)
                mm = None
                for mbl in range(8):
                    if t >= 1:
                        p.wait("tensor", "sA", hist[("sev", t - 1, mbl)])
                    for n in range(NC):
                        mm = p.op("tensor", lambda e, zt=zt, n=n, mbl=mbl: e.matmul(
                            PS[mbl][:, :], zt[:, n * 128:(n + 1) * 128],
                            memT[:, n * 4096 + mbl * 512: n * 4096 + (mbl + 1) * 512],
                            start=(n == 0), stop=(n == NC - 1)), inc="sT")
                    hist[("smm", t, mbl)] = mm
                hist[("mml", t)] = mm
                sim = sims[t % 2]
                if t >= 2:
                    p.wait("scalar", "sV", hist[("wsp", t - 2)])
                for mbl in range(8):
                    p.wait("scalar", "sT", hist[("smm", t, mbl)])
                    mE = p.op("scalar", lambda e, sim=sim, mbl=mbl: e.activation(
                        sim[:, mbl * 512:(mbl + 1) * 512], PS[mbl][:, :], AF.Copy,
                        bias=0.0, scale=RSD), inc="sA")
                    hist[("sev", t, mbl)] = mE
                p.wait("vector", "sA", hist[("sev", t, 7)])
                if t >= 1:
                    p.wait("vector", "sA", hist[("ex", t - 1)])
                p.op("vector", lambda e, sim=sim: e.max(out=sc(c_t8m, 8), in_=sim))
                p.op("vector", lambda e: e.drain())
                mNM = p.op("vector", lambda e: e.tensor_scalar(
                    sc(c_negm), small[:, c_t8m:c_t8m + 1], -1.0, None, ALU.mult), inc="sV")
                p.wait("scalar", "sV", mNM)
                if t >= 1:
                    p.wait("scalar", "sV", hist[("wsp", t - 1)])
                    p.wait("scalar", "dG", hist[("wst", t - 1)])
                mEx = p.op("scalar", lambda e, sim=sim: e.activation(
                    ew, sim, AF.Exp, bias=sc(c_negm), scale=1.0), inc="sA")
                hist[("ex", t)] = mEx
                p.wait("vector", "sA", mEx)
                if t >= 1:
                    p.wait("vector", "dG", hist[("wst", t - 1)])
                if t >= 1:
                    p.wait("vector", "dG", hist[("dst", t - 1)])
                mW = None
                for hh in range(2):
                    wsb = wsbs[hh]
                    if t >= 1:
                        p.wait("vector", "dG", hist[("wst" + str(hh), t - 1)])
                    mWh = p.op("vector", lambda e, sim=sim, hh=hh, wsb=wsb: e.scalar_tensor_tensor(
                        wsb, sim[:, hh * 2048:(hh + 1) * 2048],
                        small[:, c_t8m + 7:c_t8m + 8], ew[:, hh * 2048:(hh + 1) * 2048],
                        op0=ALU.is_ge, op1=ALU.mult), inc="sV")
                    mDh = p.op("vector", lambda e, hh=hh, wsb=wsb: e.reduce_sum(
                        small[:, c_dc + hh: c_dc + hh + 1], wsb, AXX), inc="sV")
                    p.wait("gpsimd", "sV", mDh)
                    mSt = p.dma("gpsimd",
                                wspd[t * 128:(t + 1) * 128, hh * 2048:(hh + 1) * 2048],
                                wsb, "dG")
                    p.fence("dG")
                    hist[("wst" + str(hh), t)] = mSt
                    mW = mDh
                hist[("wsp", t)] = mW
                mDC = p.op("vector", lambda e: e.tensor_tensor(
                    small[:, c_dc + 2: c_dc + 3], sc(c_dc),
                    small[:, c_dc + 1: c_dc + 2], ALU.add), inc="sV")
                p.wait("gpsimd", "sV", mDC)
                hist[("wst", t)] = hist[("wst1", t)]
                hist[("dst", t)] = p.dma("gpsimd", dend[t * 128:(t + 1) * 128, 0:1],
                                         small[:, c_dc + 2: c_dc + 3], "dG")
                p.fence("dG")
            ready["wsp"] = p.cnt["dG"]
            ready["den"] = p.cnt["dG"]

        def phase_C2():
            memS = ABF[:, 0:65536]
            wtb = [ABF[:, 65536 + i * 512: 65536 + (i + 1) * 512] for i in range(8)]
            ub = ABF[:, 69632:73728]

            p.phase_entry()
            p.wait("sync", "dG", ready["mem"])
            mMS = 0
            for i in range(NM):
                mMS = p.dma("sync", memS[:, i * 2048:(i + 1) * 2048],
                            mem_bf[i * 128:(i + 1) * 128, :], "dS")
            p.fence("dS")
            p.wait("sync", "dG", ready["wsp"])

            hist = {}
            G = 0
            for t in range(NT512):
                for half in range(2):
                    hidx = t * 2 + half
                    for mg in range(NM // 4):
                        Gg = hidx * (NM // 4) + mg
                        if Gg >= 2:
                            p.wait("sync", "sT", hist[("wmm", Gg - 2)])
                        mWT = 0
                        for mi in range(4):
                            m = mg * 4 + mi
                            wt = wtb[(m % 8)]
                            mWT = p.dma("sync", wt,
                                        wspd[t * 512:(t + 1) * 512, m * 128:(m + 1) * 128],
                                        "dS", transpose=True)
                        p.fence("dS")
                        p.wait("tensor", "dS", mMS)
                        p.wait("tensor", "dS", mWT)
                        if mg == 0 and hidx >= 1:
                            for j in range(8):
                                p.wait("tensor", "sA", hist[("uev", hidx - 1, j)])
                        mm = None
                        for mi in range(4):
                            m = mg * 4 + mi
                            wt = wtb[(m % 8)]
                            for j in range(8):
                                dj = half * 8 + j
                                mm = p.op("tensor", lambda e, wt=wt, m=m, dj=dj, j=j: e.matmul(
                                    PS[j][:, :],
                                    memS[:, m * 2048 + dj * 128: m * 2048 + (dj + 1) * 128],
                                    wt, start=(m == 0), stop=(m == NM - 1)), inc="sT")
                        hist[("wmm", Gg)] = mm
                        G += 1
                    for j in range(8):
                        dj = half * 8 + j
                        p.wait("scalar", "sT", hist[("wmm", hidx * (NM // 4) + NM // 4 - 1)])
                        if hidx >= 1:
                            p.wait("scalar", "dG", hist[("ust", hidx - 1, j)])
                        mE = p.op("scalar", lambda e, j=j: e.activation(
                            ub[:, j * 512:(j + 1) * 512], PS[j][:, :], AF.Copy), inc="sA")
                        hist[("uev", hidx, j)] = mE
                        p.wait("gpsimd", "sA", mE)
                        hist[("ust", hidx, j)] = p.dma(
                            "gpsimd", ufd[dj * 128:(dj + 1) * 128, t * 512:(t + 1) * 512],
                            ub[:, j * 512:(j + 1) * 512], "dG")
                    p.fence("dG")
            ready["u"] = p.cnt["dG"]

        def phase_D1():
            WvT = ABF[:, 0:32768]
            ubfs = [ABF[:, 32768 + i * 8192: 32768 + (i + 1) * 8192] for i in range(2)]
            cbf = ABF[:, 49152:57344]
            cdvs = [ABF[:, 57344 + i * 8192: 57344 + (i + 1) * 8192] for i in range(2)]
            prv = AFP[:, 9728:10240].bitcast(BF16)
            prb = [prv[:, 0:512], prv[:, 512:1024]]
            ptmp = AFP[:, 10240:10496].bitcast(BF16)
            grv = AFP[:, 10496:11008].bitcast(BF16)
            grbs = [grv[:, 0:512], grv[:, 512:1024]]
            dinv = AFP[0:1, 0:8192]
            dr_t = AFP[0:1, 8192:8704]
            g_t = AFP[0:1, 8704:9216]
            gr_t = AFP[0:1, 9216:9728]

            p.phase_entry()
            p.wait("sync", "dG", ready["wv"])
            mWv = 0
            for i in range(NC):
                mWv = p.dma("sync", WvT[:, i * 2048:(i + 1) * 2048],
                            wv_bf[:, i * 128:(i + 1) * 128], "dS", transpose=True)
            p.fence("dS")
            p.wait("sync", "dG", ready["u"])
            p.wait("sync", "dG", ready["ctx"])
            p.wait("sync", "dG", ready["dloc"])
            p.wait("sync", "dG", ready["den"])

            hist = {}
            mU = {}
            for t in range(NT512 + 1):
                if t < NT512:
                    if t >= 1:
                        p.wait("sync", "sT", hist[("dlbl", t - 1)])
                    mDL = 0
                    for h in range(NH):
                        mDL = p.dma("sync", dinv[0:1, h * 512:(h + 1) * 512],
                                    dlocd[h:h + 1, t * 512:(t + 1) * 512], "dS")
                    if t >= 1:
                        p.wait("sync", "sV", hist[("grtt", t - 1)])
                    mDR = p.dma("sync", dr_t, dend[t * 512:(t + 1) * 512, 0:1].rearrange("a b -> b a"), "dS")
                    p.fence("dS")
                    if t >= 2:
                        p.wait("sync", "sT", hist[("amml", t - 2)])
                    mUc = 0
                    for i in range(NC):
                        mUc = p.dma("sync", ubfs[t % 2][:, i * 512:(i + 1) * 512],
                                    ufd[i * 128:(i + 1) * 128, t * 512:(t + 1) * 512], "dS")
                    mU[t] = mUc
                    p.fence("dS")
                    if t >= 1:
                        p.wait("sync", "sV", hist[("cdvl", t - 1)])
                    mC = 0
                    for i in range(NC):
                        mC = p.dma("sync", cbf[:, i * 512:(i + 1) * 512],
                                   ctxd[i * 128:(i + 1) * 128, t * 512:(t + 1) * 512], "dS")
                    p.fence("dS")
                    p.wait("vector", "dS", mDL)
                    mRec = p.op("vector", lambda e: e.reciprocal(dinv, dinv), inc="sV")

                def em_dlb(j):
                    bank = 5 if j % 2 == 0 else 7
                    gidx = t * NC + j
                    if j == 0:
                        p.wait("tensor", "sV", mRec)
                    if gidx >= 2:
                        p.wait("tensor", "sV", hist[("cdv", gidx - 2)])
                    mm = p.op("tensor", lambda e, j=j, bank=bank: e.matmul(
                        PS[bank][:, :], ones_rf[:, :],
                        dinv[0:1, j * 512:(j + 1) * 512],
                        start=True, stop=True), inc="sT")
                    hist[("dlb", gidx)] = mm

                def em_cdv(j, mC=None, mRec=None):
                    bank = 5 if j % 2 == 0 else 7
                    gidx = t * NC + j
                    cdv = cdvs[t % 2]
                    p.wait("vector", "sT", hist[("dlb", gidx)])
                    if j == 0:
                        p.wait("vector", "dS", mC)
                    mm = p.op("vector", lambda e, j=j, bank=bank, cdv=cdv: e.tensor_tensor(
                        cdv[:, j * 512:(j + 1) * 512], PS[bank][:, :],
                        cbf[:, j * 512:(j + 1) * 512], ALU.mult), inc="sV")
                    hist[("cdv", gidx)] = mm

                def em_gm(j):
                    cdv = cdvs[t % 2]
                    p.wait("tensor", "sV", hist[("cdv", t * NC + j)])
                    if j == 0 and t >= 1:
                        p.wait("tensor", "sA", hist[("gex", t - 1)])
                    mm = p.op("tensor", lambda e, j=j, cdv=cdv: e.matmul(
                        PS[6][0:1, :], gw2[:, j:j + 1], cdv[:, j * 512:(j + 1) * 512],
                        start=(j == 0), stop=(j == NC - 1)), inc="sT")
                    hist[("gm", t, j)] = mm

                def em_pass2(tp, j):
                    gidx = tp * NC + j
                    bank = j % 4
                    ubf = ubfs[tp % 2]
                    cdv = cdvs[tp % 2]
                    grb = grbs[tp % 2]
                    if j == 0:
                        p.wait("tensor", "dS", mWv)
                        p.wait("tensor", "dS", mU[tp])
                    if gidx >= 4:
                        p.wait("tensor", "sV", hist[("pre", gidx - 4)])
                    mm = None
                    for dp in range(NC):
                        mm = p.op("tensor", lambda e, dp=dp, j=j, bank=bank, ubf=ubf: e.matmul(
                            PS[bank][:, :],
                            WvT[:, dp * 2048 + j * 128: dp * 2048 + (j + 1) * 128],
                            ubf[:, dp * 512:(dp + 1) * 512],
                            start=(dp == 0), stop=(dp == NC - 1)), inc="sT")
                    hist[("amm", gidx)] = mm
                    pb = prb[gidx % 2]
                    p.wait("vector", "sT", mm)
                    if j == 0:
                        p.wait("vector", "sV", hist[("grbev", tp)])
                    if gidx >= 2:
                        p.wait("vector", "dG", hist[("pst", gidx - 2)])
                    mPt = p.op("vector", lambda e, bank=bank, grb=grb: e.tensor_tensor(
                        ptmp, PS[bank][:, :], grb, ALU.mult), inc="sV")
                    p.wait("vector", "sV", mPt)
                    mPre = p.op("vector", lambda e, pb=pb, j=j, cdv=cdv: e.tensor_tensor(
                        pb, ptmp, cdv[:, j * 512:(j + 1) * 512], ALU.add), inc="sV")
                    hist[("pre", gidx)] = mPre
                    p.wait("gpsimd", "sV", mPre)
                    hist[("pst", gidx)] = p.dma(
                        "gpsimd", pred[j * 128:(j + 1) * 128, tp * 512:(tp + 1) * 512],
                        pb, "dG")
                    p.fence("dG")

                for j in range(NC + 2):
                    if t >= 1 and j < NC:
                        em_pass2(t - 1, j)
                    if t < NT512:
                        if j < NC:
                            em_dlb(j)
                        if 1 <= j <= NC:
                            em_cdv(j - 1, mC, mRec)
                        if j >= 2:
                            em_gm(j - 2)
                if t >= 1:
                    hist[("amml", t - 1)] = hist[("amm", (t - 1) * NC + NC - 1)]

                if t < NT512:
                    hist[("dlbl", t)] = hist[("dlb", t * NC + NC - 1)]
                    hist[("cdvl", t)] = hist[("cdv", t * NC + NC - 1)]
                    hist[("gml", t)] = hist[("gm", t, NC - 1)]
                    p.wait("scalar", "sT", hist[("gml", t)])
                    if t >= 1:
                        p.wait("scalar", "sV", hist[("grtt", t - 1)])
                    mge = p.op("scalar", lambda e: e.activation(
                        g_t, PS[6][0:1, :], AF.Exp, bias=negb[0:1, 0:1], scale=-1.0), inc="sA")
                    hist[("gex", t)] = mge
                    p.wait("vector", "sA", mge)
                    p.op("vector", lambda e: e.tensor_scalar(g_t, g_t, 1.0, None, ALU.add), inc="sV")
                    p.op("vector", lambda e: e.reciprocal(g_t, g_t), inc="sV")
                    p.wait("vector", "dS", mDR)
                    p.op("vector", lambda e: e.reciprocal(dr_t, dr_t), inc="sV")
                    if t >= 1:
                        p.wait("vector", "sT", hist[("bcm", t - 1)])
                    mRow = p.op("vector", lambda e: e.tensor_tensor(
                        gr_t, dr_t, g_t, ALU.mult), inc="sV")
                    hist[("grtt", t)] = mRow
                    p.wait("tensor", "sV", mRow)
                    if t >= 1:
                        p.wait("tensor", "sV", hist[("grbev", t - 1)])
                    mBC = p.op("tensor", lambda e: e.matmul(
                        PS[4][:, :], ones_rf[:, :], gr_t, start=True, stop=True), inc="sT")
                    hist[("bcm", t)] = mBC
                    p.wait("vector", "sT", mBC)
                    mGRB = p.op("vector", lambda e, t=t: e.tensor_copy(
                        grbs[t % 2], PS[4][:, :]), inc="sV")
                    hist[("grbev", t)] = mGRB
            ready["pre"] = p.cnt["dG"]

        def phase_D2():
            WoT = ABF[:, 0:32768]
            pfm = [ABF[:, 32768 + i * 8192: 32768 + (i + 1) * 8192] for i in range(2)]
            xb = [AFP[:, i * 2048:(i + 1) * 2048] for i in range(2)]
            x2sb = [AFP[:, 4096 + i * 2048: 4096 + (i + 1) * 2048] for i in range(2)]

            p.phase_entry()
            p.wait("sync", "dG", ready["wo"])
            mWo = 0
            for i in range(NC):
                mWo = p.dma("sync", WoT[:, i * 2048:(i + 1) * 2048],
                            wo_bf[:, i * 128:(i + 1) * 128], "dS", transpose=True)
            p.fence("dS")
            p.wait("sync", "dG", ready["pre"])
            p.barrier_into("gpsimd")

            hist = {}
            g = 0
            for t in range(NT512):
                pf = pfm[t % 2]
                if t >= 2:
                    p.wait("sync", "sT", hist[("mml", t - 2)])
                mPf = 0
                for i in range(NC):
                    mPf = p.dma("sync", pf[:, i * 512:(i + 1) * 512],
                                pred[i * 128:(i + 1) * 128, t * 512:(t + 1) * 512], "dS")
                p.fence("dS")
                p.wait("tensor", "dS", mWo)
                p.wait("tensor", "dS", mPf)
                for s in range(4):
                    ts_ = t * 4 + s
                    xt = xb[ts_ % 2]
                    if ts_ >= 2:
                        p.wait("gpsimd", "sV", hist[("x2", ts_ - 2)])
                    mX = p.dma("gpsimd", xt, x1d[ts_ * 128:(ts_ + 1) * 128, :], "dG")
                    p.fence("dG")
                    x2t = x2sb[ts_ % 2]
                    mRe = None
                    for obk in range(4):
                        bank = g % 4
                        if g >= 4:
                            p.wait("tensor", "sV", hist[("oev", g - 4)])
                        mm = None
                        for c in range(NC):
                            mm = p.op("tensor", lambda e, s=s, obk=obk, c=c, pf=pf, bank=bank: e.matmul(
                                PS[bank][:, :],
                                pf[:, c * 512 + s * 128: c * 512 + (s + 1) * 128],
                                WoT[:, c * 2048 + obk * 512: c * 2048 + (obk + 1) * 512],
                                start=(c == 0), stop=(c == NC - 1)), inc="sT")
                        hist[("mm", g)] = mm
                        p.wait("vector", "sT", mm)
                        p.wait("vector", "dG", mX)
                        if ts_ >= 2:
                            p.wait("vector", "dG", hist[("x2st", ts_ - 2)])
                        mRe = p.op("vector", lambda e, obk=obk, x2t=x2t, xt=xt, bank=bank:
                                   e.tensor_tensor(
                                       x2t[:, obk * 512:(obk + 1) * 512], PS[bank][:, :],
                                       xt[:, obk * 512:(obk + 1) * 512], ALU.add), inc="sV")
                        hist[("oev", g)] = mRe
                        g += 1
                    hist[("x2", ts_)] = mRe
                    p.wait("gpsimd", "sV", mRe)
                    hist[("x2st", ts_)] = p.dma(
                        "gpsimd", x2d[ts_ * 128:(ts_ + 1) * 128, :], x2t, "dG")
                    p.fence("dG")
                hist[("mml", t)] = hist[("mm", g - 1)]
            ready["x2"] = p.cnt["dG"]

        ph = {
            "P0": phase_P0,
            "P1": lambda: phase_FFN(xd, x1d, h1d, "wf", False, entry=False),
            "L2": lambda: (emit_cast("wq", wq_bf, wq_in),
                           emit_cast("wk", wk_bf, wk_in),
                           preload_projW("wq", wq_bf),
                           phase_L2(x1d, h2d)),
            "Aq": lambda: (emit_cast("wv", wv_bf, wv_in),
                           phase_proj("q", wq_bf, qfd, entry=False)),
            "Ak": lambda: (emit_cast("mem", mem_bf, mem_in),
                           phase_proj("k", wk_bf, kfd)),
            "Av": lambda: phase_proj("v", wv_bf, vtd),
            "Az": lambda: phase_proj("z", wk_bf, zfd),
            "B": phase_B,
            "C1": phase_C1,
            "C2": phase_C2,
            "D1": phase_D1,
            "D2": phase_D2,
            "E": lambda: phase_FFN(x2d, yd, h3d, "wf", False),
        }
        for name in PHASES:
            p.phase = name
            ph[name]()

        p.wait("gpsimd", "dG", p.cnt["dG"])
        p.wait("sync", "dS", p.cnt["dS"])

        with nc.Block() as block:
            p.run(block)
    return nc


def kernel(L0_attn_hidden_states, L0_attn_wq, L0_attn_wk, L0_attn_wv,
           L0_attn_wo, L0_attn_gate_w, L0_attn_gate_b, L0_attn_mem_bank,
           L0_ffn_weight, L0_ffn_scale):
    nc = build_nc()
    x = np.ascontiguousarray(np.asarray(L0_attn_hidden_states, dtype=np.float32))
    common = {
        "wq": np.ascontiguousarray(np.asarray(L0_attn_wq, np.float32)),
        "wk": np.ascontiguousarray(np.asarray(L0_attn_wk, np.float32)),
        "wv": np.ascontiguousarray(np.asarray(L0_attn_wv, np.float32)),
        "wo": np.ascontiguousarray(np.asarray(L0_attn_wo, np.float32)),
        "gate_w": np.ascontiguousarray(np.asarray(L0_attn_gate_w, np.float32).reshape(1, D)),
        "gate_b": np.ascontiguousarray(np.asarray(L0_attn_gate_b, np.float32).reshape(1, 1)),
        "mem_bank": np.ascontiguousarray(np.asarray(L0_attn_mem_bank, np.float32)),
        "ffn_w": np.ascontiguousarray(np.asarray(L0_ffn_weight, np.float32)),
        "ffn_scale": np.ascontiguousarray(np.asarray(L0_ffn_scale, np.float32).reshape(1, D)),
    }
    in_maps = [dict(common, x=np.ascontiguousarray(x[b])) for b in range(8)]
    res = run_bass_kernel_spmd(nc, in_maps, core_ids=list(range(8)))
    out = np.stack([np.asarray(res.results[b]["y"]) for b in range(8)], axis=0)
    return out


if __name__ == "__main__":
    rng = np.random.default_rng(0)
    args = dict(
        L0_attn_hidden_states=rng.standard_normal((8, T, D), dtype=np.float32),
        L0_attn_wq=rng.standard_normal((D, D), dtype=np.float32) / 45,
        L0_attn_wk=rng.standard_normal((D, D), dtype=np.float32) / 45,
        L0_attn_wv=rng.standard_normal((D, D), dtype=np.float32) / 45,
        L0_attn_wo=rng.standard_normal((D, D), dtype=np.float32) / 45,
        L0_attn_gate_w=rng.standard_normal((1, D), dtype=np.float32) / 45,
        L0_attn_gate_b=np.full((1,), -2.0, np.float32),
        L0_attn_mem_bank=rng.standard_normal((M, D), dtype=np.float32),
        L0_ffn_weight=rng.standard_normal((D, D), dtype=np.float32),
        L0_ffn_scale=np.full((D,), 1 / 45.25, np.float32),
    )
    y = kernel(**args)
    print(y.shape, y.dtype, np.abs(y).mean())
